# version 18 (final)
# baseline (speedup 1.0000x reference)
import contextlib
import math

import numpy as np
import ml_dtypes

import concourse.bass as bass
import concourse.mybir as mybir
from concourse.bass_utils import run_bass_kernel_spmd

F32 = mybir.dt.float32
BF16 = mybir.dt.bfloat16
I32 = mybir.dt.int32
AF = mybir.ActivationFunctionType
ALU = mybir.AluOpType
AX = mybir.AxisListType
BB = ml_dtypes.bfloat16

NCORES = 8
D = 1024
CC = 8
T = 4096
TC = 512
NCH = T // TC
NB = 2
S = 2048
GSZ = 256
NGRP = T // GSZ
HID = 2048
QKD = 128
NE = 16
M = 512
FFH = 3072
NHT = FFH // 128
NHEAD = 16
DH = 64
L1SCALE = 1.0 / 8.0
LNEPS = 1e-5
MOE_SCALE = 2.5
DEQ = 1.0 / 127.0

DEV = False
RELAX_LEVEL = 0

ENGS = ("sync", "gpsimd", "vector", "scalar", "tensor")
OWNSEM = {"tensor": "TE", "vector": "VE", "scalar": "SE", "gpsimd": "GE"}
ALLSEMS = ("TE", "VE", "SE", "GE")


class KB:

    def __init__(self, nc, es):
        self.nc = nc
        self._es = es
        self.cur = "__dry__"
        self.eng = None
        self.store = {}
        self.alloc_seq = []
        self.semh = {}
        self.dma_sems = []
        self.cnt = {}
        self.wd = {}
        self.prev_small = {}

    def walk_reset(self):
        self.cnt = {s: 0 for s in ALLSEMS}
        for s in self.dma_sems:
            self.cnt[s] = 0
        self.wd = {e: {} for e in ENGS}
        self.prev_small = {}

    def sbuf(self, name, shape, dtype):
        if name not in self.store:
            assert self.cur == "__dry__", f"late alloc {name}"
            g = self.nc.sbuf_tensor(name, list(shape), dtype)
            h = g.__enter__()
            self.store[name] = (g, h)
            self.alloc_seq.append(name)
        return self.store[name][1]

    def psum(self, name, shape, dtype=F32):
        if name not in self.store:
            assert self.cur == "__dry__", f"late alloc {name}"
            g = self.nc.psum_tensor(name, list(shape), dtype)
            h = g.__enter__()
            self.store[name] = (g, h)
            self.alloc_seq.append(name)
        return self.store[name][1]

    def free(self, *names):
        if self.cur == "__dry__":
            order = {n: i for i, n in enumerate(self.alloc_seq)}
            for n in sorted(names, key=lambda n: -order[n]):
                g, _ = self.store[n]
                g.__exit__(None, None, None)

    def wait(self, engine, pt):
        if pt is None:
            return
        sem, val = pt
        if OWNSEM.get(engine) == sem:
            return
        if sem not in self.semh:
            assert self.cur == "__dry__", f"late sem {sem}"
            self.semh[sem] = self._es.enter_context(self.nc.semaphore(sem))
            self.dma_sems.append(sem)
        if val <= self.wd[engine].get(sem, 0):
            return
        self.wd[engine][sem] = val
        if self.cur == engine:
            self.eng.wait_ge(self.semh[sem], val)

    def barrier(self, engines=ENGS):
        pts = [(s, v) for s, v in self.cnt.items() if v > 0 and s != "PREP"]
        for e in engines:
            for pt in pts:
                self.wait(e, pt)

    def dma(self, q, out, in_, sem, wait=()):
        for pt in wait:
            self.wait(q, pt)
        if sem not in self.semh:
            assert self.cur == "__dry__", f"late sem {sem}"
            self.semh[sem] = self._es.enter_context(self.nc.semaphore(sem))
            self.dma_sems.append(sem)
        if sem not in self.cnt:
            self.cnt[sem] = 0
        n = 32 if ("DRam" in type(out.tensor).__name__
                   and "DRam" in type(in_.tensor).__name__) else 16
        self.cnt[sem] += n
        if self.cur == q:
            self.eng.dma_start(out=out, in_=in_).then_inc(self.semh[sem], n)
        return (sem, self.cnt[sem])

    def op(self, engine, opname, *a, wait=(), swait=(), inc=False,
           const_scalars=False, **kw):
        for pt in wait:
            self.wait(engine, pt)
        for pt in swait:
            if pt is None:
                continue
            sem_, val_ = pt
            if val_ > self.wd[engine].get(sem_, 0):
                self.wd[engine][sem_] = val_
                if self.cur == engine:
                    self.eng.wait_ge(self.semh[sem_], val_)
        sem = OWNSEM[engine]
        if engine in ("vector", "scalar", "gpsimd"):
            def _ap(x):
                return hasattr(x, "tensor")
            apscal = (not const_scalars) and any(
                _ap(kw.get(k)) for k in ("scalar1", "scalar2", "bias", "scale"))
            need = apscal or self.prev_small.get(engine, False)
            out_ap = kw.get("out")
            if out_ap is None and a:
                out_ap = a[0]
            try:
                import math as _m
                elems = 1
                for s_ in out_ap.shape[1:]:
                    elems *= s_
            except Exception:
                elems = 0
            self.prev_small[engine] = elems < 256 or (
                getattr(out_ap, "shape", (128,))[0] < 32)
            prev = self.cnt[sem]
            self.cnt[sem] += 1
            pt_out = (sem, self.cnt[sem])
            if self.cur == engine:
                if need and prev > 0:
                    self.eng.wait_ge(self.semh[sem], prev)
                ins = getattr(self.eng, opname)(*a, **kw)
                ins.then_inc(self.semh[sem], 1)
            return pt_out
        pt_out = None
        if inc:
            self.cnt[sem] += 1
            pt_out = (sem, self.cnt[sem])
        if self.cur == engine:
            ins = getattr(self.eng, opname)(*a, **kw)
            if inc:
                ins.then_inc(self.semh[sem], 1)
        return pt_out

    def mm(self, out, lhsT, rhs, start, stop, wait=(), inc=False):
        return self.op("tensor", "matmul", wait=wait, inc=inc,
                       out=out, lhsT=lhsT, rhs=rhs, start=start, stop=stop)

    def tr(self, out, in_, ident, wait=(), inc=False):
        return self.op("tensor", "transpose", wait=wait, inc=inc,
                       out=out, in_=in_, identity=ident)

    def act(self, out, in_, func, bias=0.0, scale=1.0, wait=(), inc=False,
            const_scalars=True):
        return self.op("scalar", "activation", wait=wait, inc=inc,
                       const_scalars=const_scalars,
                       out=out, in_=in_, func=func, bias=bias, scale=scale)


def _rel_bias_matrix(rel_emb):
    NUM_BUCKETS, MAX_DIST = 32, 128
    pos = np.arange(GSZ)
    n = -(pos[None, :] - pos[:, None])
    nb = NUM_BUCKETS // 2
    ret = (n < 0).astype(np.int64) * nb
    n = np.abs(n)
    max_exact = nb // 2
    is_small = n < max_exact
    val_large = max_exact + (
        np.log(np.maximum(n, 1).astype(np.float32) / max_exact)
        / math.log(MAX_DIST / max_exact) * (nb - max_exact)
    ).astype(np.int64)
    val_large = np.minimum(val_large, nb - 1)
    bucket = ret + np.where(is_small, n, val_large)
    return rel_emb[bucket, 0]


def host_prep(inp):
    h = {}

    def lhsT_layout(w, name):
        C, F = w.shape
        h[name] = np.ascontiguousarray(
            w.reshape(C // 128, 128, F // 128, 128).transpose(1, 0, 2, 3))

    def rhs_layout(w, name):
        C, F = w.shape
        h[name] = np.ascontiguousarray(w.reshape(C // 128, 128, F))

    lhsT_layout(inp["L0_attn_W_qk"], "wqk")
    rhs_layout(inp["L0_attn_W_h"][:, :HID], "whv")
    lhsT_layout(inp["L0_attn_W_h"][:, HID:], "whg")
    lhsT_layout(inp["L0_attn_W_o"], "wo")
    h["bhv"] = np.ascontiguousarray(inp["L0_attn_b_h"][:HID].reshape(1, HID))
    h["bhg"] = np.ascontiguousarray(inp["L0_attn_b_h"][HID:].reshape(16, 128).T)
    h["bqk"] = np.ascontiguousarray(inp["L0_attn_b_qk"].reshape(1, 128).T)
    h["qkg"] = np.ascontiguousarray(inp["L0_attn_qk_gamma"].T)
    h["qkb"] = np.ascontiguousarray(inp["L0_attn_qk_beta"].T)
    h["lng"] = np.ascontiguousarray(inp["L0_attn_norm_g"].reshape(8, 128).T)
    h["lnb"] = np.ascontiguousarray(inp["L0_attn_norm_b"].reshape(8, 128).T)
    h["bo"] = np.ascontiguousarray(inp["L0_attn_b_o"].reshape(8, 128).T)
    bT = _rel_bias_matrix(np.asarray(inp["L0_attn_rel_emb"])).T
    h["biasT"] = np.ascontiguousarray(bT.reshape(2, 128, 256).transpose(1, 0, 2))

    h["gwT"] = np.ascontiguousarray(
        inp["L0_ffn_gate_w"].T.reshape(8, 128, 16).transpose(1, 0, 2))

    def expw_gu(w):
        return w.reshape(8, 128, 4, 128).transpose(1, 2, 0, 3)

    def expw_d(w):
        return w.reshape(4, 128, 8, 128).transpose(1, 2, 0, 3)

    h["mwg"] = np.ascontiguousarray(np.stack([expw_gu(inp["L0_ffn_w_gate"][e]) for e in range(NE)]))
    h["mwu"] = np.ascontiguousarray(np.stack([expw_gu(inp["L0_ffn_w_up"][e]) for e in range(NE)]))
    h["mwd"] = np.ascontiguousarray(np.stack([expw_d(inp["L0_ffn_w_down"][e]) for e in range(NE)]))
    h["wsg"] = np.ascontiguousarray(expw_gu(inp["L0_ffn_ws_gate"]))
    h["wsu"] = np.ascontiguousarray(expw_gu(inp["L0_ffn_ws_up"]))
    h["wsd"] = np.ascontiguousarray(expw_d(inp["L0_ffn_ws_down"]))

    lhsT_layout(inp["L1_attn_Wq"], "wq1")
    lhsT_layout(inp["L1_attn_Wk"], "wk1")
    rhs_layout(inp["L1_attn_Wv"], "wv1")
    lhsT_layout(inp["L1_attn_Wo"], "wo1")
    h["gam1"] = np.ascontiguousarray(inp["L1_attn_gamma"].reshape(8, 128).T)
    h["bet1"] = np.ascontiguousarray(inp["L1_attn_beta"].reshape(8, 128).T)

    wgT = inp["L1_ffn_w_gate"].T
    wuT = inp["L1_ffn_w_up"].T
    wdT = inp["L1_ffn_w_down"].T
    h["fg"] = np.ascontiguousarray(wgT.reshape(8, 128, 24, 128).transpose(2, 1, 0, 3))
    h["fu"] = np.ascontiguousarray(wuT.reshape(8, 128, 24, 128).transpose(2, 1, 0, 3))
    h["fd"] = np.ascontiguousarray(wdT.reshape(24, 128, 8, 128).transpose(2, 1, 0, 3))

    h["idf"] = np.eye(128, dtype=np.float32)
    h["idb"] = np.eye(128).astype(BB)
    h["onesb"] = np.ones((128, 1), dtype=BB)
    h["onesf"] = np.ones((1, 128), dtype=np.float32)
    h["onesr2k"] = np.ones((1, S), dtype=BB)
    return h



def declare_tensors(nc):
    d = {}

    def din(name, shape, dtype):
        d[name] = nc.dram_tensor(name, list(shape), dtype, kind="ExternalInput").ap()

    def dscr(name, shape, dtype):
        d[name] = nc.dram_tensor(name, list(shape), dtype, kind="Internal").ap()

    din("xT", (CC, 128, T), F32)
    din("wqk", (128, 8, 1, 128), F32)
    din("whv", (8, 128, HID), F32)
    din("whg", (128, 8, 16, 128), F32)
    din("wo", (128, 16, 8, 128), F32)
    din("bhv", (1, HID), F32)
    din("bhg", (128, 16), F32)
    din("bqk", (128, 1), F32)
    din("qkg", (128, 4), F32)
    din("qkb", (128, 4), F32)
    din("lng", (128, 8), F32)
    din("lnb", (128, 8), F32)
    din("bo", (128, 8), F32)
    din("biasT", (128, 2, 256), F32)
    din("gwT", (128, 8, 16), F32)
    din("mwg", (NE, 128, 4, 8, 128), F32)
    din("mwu", (NE, 128, 4, 8, 128), F32)
    din("mwd", (NE, 128, 8, 4, 128), F32)
    din("wsg", (128, 4, 8, 128), F32)
    din("wsu", (128, 4, 8, 128), F32)
    din("wsd", (128, 8, 4, 128), F32)
    din("wq1", (128, 8, 8, 128), F32)
    din("wk1", (128, 8, 8, 128), F32)
    din("wv1", (8, 128, 1024), F32)
    din("wo1", (128, 8, 8, 128), F32)
    din("gam1", (128, 8), F32)
    din("bet1", (128, 8), F32)
    din("fg", (NHT, 128, 8, 128), I32)
    din("fu", (NHT, 128, 8, 128), I32)
    din("fd", (8, 128, NHT, 128), I32)
    din("idf", (128, 128), F32)
    din("idb", (128, 128), BF16)
    din("onesb", (128, 1), BF16)
    din("onesf", (1, 128), F32)
    din("onesr2k", (1, S), BF16)

    d["out"] = nc.dram_tensor("out", [CC, 128, T], F32, kind="ExternalOutput").ap()

    dscr("s_wqk", (128, 8, 1, 128), BF16)
    dscr("s_whv", (8, 128, HID), BF16)
    dscr("s_whg", (128, 8, 16, 128), BF16)
    dscr("s_wo", (128, 16, 8, 128), BF16)
    dscr("s_bhv", (1, HID), BF16)
    dscr("s_mwg", (NE, 128, 4, 8, 128), BF16)
    dscr("s_mwu", (NE, 128, 4, 8, 128), BF16)
    dscr("s_mwd", (NE, 128, 8, 4, 128), BF16)
    dscr("s_wsg", (128, 4, 8, 128), BF16)
    dscr("s_wsu", (128, 4, 8, 128), BF16)
    dscr("s_wsd", (128, 8, 4, 128), BF16)
    dscr("s_wq1", (128, 8, 8, 128), BF16)
    dscr("s_wk1", (128, 8, 8, 128), BF16)
    dscr("s_wv1", (8, 128, 1024), BF16)
    dscr("s_wo1", (128, 8, 8, 128), BF16)
    dscr("s_fg", (NHT, 128, 8, 128), BF16)
    dscr("s_fu", (NHT, 128, 8, 128), BF16)
    dscr("s_fd", (8, 128, NHT, 128), BF16)

    dscr("s_route", (16, T), F32)
    dscr("s_vt", (NB, 16, 128, HID), BF16)
    dscr("s_crow", (2, 16, 128), BF16)
    dscr("s_mr", (3, T), F32)
    dscr("s_mr1", (2, T), F32)
    dscr("s_xnb", (CC, 128, T), BF16)
    dscr("s_nmb", (CC, 128, T), BF16)
    dscr("s_moe", (CC, 128, T), F32)
    dscr("s_x1", (CC, 128, T), F32)
    dscr("s_aout", (CC, 128, T), F32)
    dscr("s_xn1b", (CC, 128, T), BF16)

    if DEV:
        d["dbg_xn"] = nc.dram_tensor("dbg_xn", [CC, 128, T], F32, kind="ExternalOutput").ap()
        d["dbg_route"] = nc.dram_tensor("dbg_route", [16, T], F32, kind="ExternalOutput").ap()
        d["dbg_nmb"] = nc.dram_tensor("dbg_nmb", [CC, 128, T], F32, kind="ExternalOutput").ap()
        d["dbg_qk"] = nc.dram_tensor("dbg_qk", [128, T], F32, kind="ExternalOutput").ap()
        d["dbg_scs"] = nc.dram_tensor("dbg_scs", [16, T], F32, kind="ExternalOutput").ap()
        d["dbg_stok"] = nc.dram_tensor("dbg_stok", [128, 16], F32, kind="ExternalOutput").ap()
        d["dbg_rtok"] = nc.dram_tensor("dbg_rtok", [128, 16], F32, kind="ExternalOutput").ap()
        d["dbg_x1"] = nc.dram_tensor("dbg_x1", [CC, 128, T], F32, kind="ExternalOutput").ap()
        d["dbg_x1f"] = nc.dram_tensor("dbg_x1f", [CC, 128, T], F32, kind="ExternalOutput").ap()
    return d


def phase_prep(K, d):
    pts = {}
    pts["wqk"] = K.dma("gpsimd", d["s_wqk"][:], d["wqk"][:], "pWQK")
    pts["bhv"] = K.dma("gpsimd", d["s_bhv"][:], d["bhv"][:], "pBHV")
    pts["whv"] = K.dma("gpsimd", d["s_whv"][:], d["whv"][:], "pWHV")
    pts["whg"] = K.dma("gpsimd", d["s_whg"][:], d["whg"][:], "pWHG")
    pts["wo"] = K.dma("gpsimd", d["s_wo"][:], d["wo"][:], "pWO")
    for e in range(NE):
        gate = [("aSTn", 16 * min(NCH, max(1, e)))]
        K.dma("gpsimd", d["s_mwg"][e], d["mwg"][e], "PREP", wait=gate)
        K.dma("gpsimd", d["s_mwu"][e], d["mwu"][e], "PREP")
        p3 = K.dma("gpsimd", d["s_mwd"][e], d["mwd"][e], "PREP")
        pts[f"moe{e}"] = p3
    K.dma("gpsimd", d["s_wsg"][:], d["wsg"][:], "PREP")
    K.dma("gpsimd", d["s_wsu"][:], d["wsu"][:], "PREP")
    pts["moeS"] = K.dma("gpsimd", d["s_wsd"][:], d["wsd"][:], "PREP")
    gate1 = [("bAC0", 32)]
    K.dma("gpsimd", d["s_wq1"][:], d["wq1"][:], "PREP", wait=gate1)
    K.dma("gpsimd", d["s_wk1"][:], d["wk1"][:], "PREP")
    K.dma("gpsimd", d["s_wv1"][:], d["wv1"][:], "PREP")
    K.dma("gpsimd", d["s_wo1"][:], d["wo1"][:], "PREP")
    K.dma("gpsimd", d["s_fg"][:], d["fg"][:], "PREP")
    K.dma("gpsimd", d["s_fu"][:], d["fu"][:], "PREP")
    pts["all"] = K.dma("gpsimd", d["s_fd"][:], d["fd"][:], "PREP")
    pts["PREP"] = ("PREP", K.cnt["PREP"])
    return pts

def phase_A(K, d, wp):
    onesb = K.sbuf("c_onesb", (128, 1), BF16)
    onesf = K.sbuf("c_onesf", (1, 128), F32)
    idf = K.sbuf("c_idf", (128, 128), F32)
    gwT = K.sbuf("c_gwT", (128, 8, 16), F32)
    wqk = K.sbuf("c_wqk16", (128, 8, 1, 128), BF16)
    bqk = K.sbuf("c_bqk", (128, 1), F32)
    qkg = K.sbuf("c_qkg", (128, 4), F32)
    qkb = K.sbuf("c_qkb", (128, 4), F32)
    lng = K.sbuf("c_lng", (128, 8), F32)
    lnb = K.sbuf("c_lnb", (128, 8), F32)
    for sb, dr in ((onesb, "onesb"), (onesf, "onesf"), (idf, "idf"), (gwT, "gwT"),
                   (bqk, "bqk"), (qkg, "qkg"), (qkb, "qkb"), (lng, "lng"), (lnb, "lnb")):
        cst_pt = K.dma("sync", sb[:], d[dr][:], "aCST")
    ceps = K.sbuf("c_eps", (128, 1), F32)
    eps_pt = K.op("vector", "memset", ceps[:], LNEPS, inc=True)
    qqT = K.sbuf("qqT", (128, T), BF16)
    qkT2 = K.sbuf("qkT2", (128, T), BF16)
    lqT = K.sbuf("lqT", (128, T), BF16)
    ltok = K.sbuf("ltok", (128, T), BF16)
    wqk_pt = K.dma("sync", wqk[:], d["s_wqk"][:], "aWQK", wait=[wp["wqk"]])

    routeT = K.sbuf("routeT", (16, T), F32)
    scraw = K.sbuf("A_scraw", (16, T), F32)
    xf = [K.sbuf(f"A_xf{i}", (128, CC, TC), F32) for i in range(2)]
    xb16 = [K.sbuf(f"A_xb16{i}", (128, CC, TC), BF16) for i in range(2)]
    xsq = [K.sbuf(f"A_xsq{i}", (128, CC, TC), BF16) for i in range(2)]
    srow = [K.sbuf(f"A_srow{i}", (1, 2 * TC), F32) for i in range(2)]
    vrow = [K.sbuf(f"A_vrow{i}", (1, 5 * TC), F32) for i in range(2)]
    pst = K.psum("A_pst", (1, 4 * TC), F32)
    xT_pct = d["xT"][:].rearrange("c p t -> p c t")
    ld = [None] * NCH
    done = [None, None]
    st_mr = [None, None]
    ev_stats = [None, None]
    ld[0] = K.dma("sync", xf[0][:], xT_pct[:, :, 0:TC], "aXF0", wait=[cst_pt])
    for c in range(NCH):
        cb = c % 2
        tsl = slice(c * TC, (c + 1) * TC)
        if c + 1 < NCH:
            ld[c + 1] = K.dma("sync", xf[(c + 1) % 2][:],
                              xT_pct[:, :, (c + 1) * TC:(c + 2) * TC],
                              f"aXF{(c + 1) % 2}", wait=[done[(c + 1) % 2]])
        pcp = K.op("vector", "tensor_copy", wait=[ld[c]], inc=True,
                   out=xb16[cb][:], in_=xf[cb][:])
        psq = K.act(xsq[cb][:], xf[cb][:], AF.Square, wait=[ld[c]], inc=True)
        done[cb] = psq
        po = cb * 1024
        for i in range(CC):
            K.mm(pst[0:1, po:po + 512], onesb[:], xb16[cb][:, i, :], start=(i == 0),
                 stop=(i == CC - 1), wait=[pcp, ev_stats[cb]])
        mmst = None
        for i in range(CC):
            mmst = K.mm(pst[0:1, po + 512:po + 1024], onesb[:], xsq[cb][:, i, :],
                        start=(i == 0), stop=(i == CC - 1), wait=[psq],
                        inc=(i == CC - 1))
        K.act(srow[cb][0:1, 0:TC], pst[0:1, po:po + 512], AF.Identity,
              scale=1.0 / D, wait=[mmst, st_mr[cb]])
        ev_stats[cb] = K.act(srow[cb][0:1, TC:2 * TC], pst[0:1, po + 512:po + 1024],
                             AF.Identity, scale=1.0 / D, inc=True)
        K.dma("sync", d["s_mr"][0:1, tsl], srow[cb][0:1, 0:TC], f"aSTm{cb}",
              wait=[ev_stats[cb]])
        K.op("vector", "tensor_mul", wait=[ev_stats[cb]], out=vrow[cb][0:1, 0:TC],
             in0=srow[cb][0:1, 0:TC], in1=srow[cb][0:1, 0:TC])
        vv = K.op("vector", "tensor_sub", inc=True, out=vrow[cb][0:1, 0:TC],
                  in0=srow[cb][0:1, TC:2 * TC], in1=vrow[cb][0:1, 0:TC])
        sd = K.act(vrow[cb][0:1, TC:2 * TC], vrow[cb][0:1, 0:TC], AF.Sqrt,
                   bias=ceps[0:1, :], wait=[vv], inc=True)
        rr = K.op("vector", "reciprocal", wait=[sd], inc=True,
                  out=vrow[cb][0:1, 2 * TC:3 * TC], in_=vrow[cb][0:1, TC:2 * TC])
        K.dma("sync", d["s_mr"][1:2, tsl], vrow[cb][0:1, 2 * TC:3 * TC],
              f"aSTm{cb}", wait=[rr])
        K.op("vector", "tensor_mul", wait=[rr], out=vrow[cb][0:1, 3 * TC:4 * TC],
             in0=vrow[cb][0:1, 2 * TC:3 * TC], in1=vrow[cb][0:1, 2 * TC:3 * TC])
        v2 = K.op("vector", "tensor_mul", inc=True, out=vrow[cb][0:1, 3 * TC:4 * TC],
                  in0=vrow[cb][0:1, 3 * TC:4 * TC], in1=vrow[cb][0:1, 0:TC])
        sd2 = K.act(vrow[cb][0:1, TC:2 * TC], vrow[cb][0:1, 3 * TC:4 * TC], AF.Sqrt,
                    bias=ceps[0:1, :], wait=[v2], inc=True)
        r2 = K.op("vector", "reciprocal", wait=[sd2], inc=True,
                  out=vrow[cb][0:1, 4 * TC:5 * TC], in_=vrow[cb][0:1, TC:2 * TC])
        st_mr[cb] = K.dma("sync", d["s_mr"][2:3, tsl], vrow[cb][0:1, 4 * TC:5 * TC],
                          f"aSTm{cb}", wait=[r2])
    K.barrier(("sync", "vector", "scalar", "tensor"))
    K.free("A_xb160", "A_xb161", "A_xsq0", "A_xsq1", "A_srow0", "A_srow1",
           "A_vrow0", "A_vrow1", "A_pst")

    xnf = K.sbuf("A_xnf", (128, CC, TC), F32)
    nmp = K.sbuf("A_nmp", (128, CC, TC), F32)
    nmb = K.sbuf("A_nmb", (128, CC, TC), BF16)
    xnb = K.sbuf("A_xnb", (128, CC, TC), BF16)
    qk16 = K.sbuf("A_qk16", (128, TC), BF16)
    lkT = K.sbuf("A_lkT", (128, TC), F32)
    mr3 = [K.sbuf(f"A_mr3{i}", (1, 3 * TC), F32) for i in range(2)]
    psb = K.psum("A_psb", (128, 3 * 512), F32)
    pssc = K.psum("A_pssc", (16, 512), F32)
    psqk = K.psum("A_psqk", (128, 512), F32)
    pstr = K.psum("A_pstr", (128, 128), F32)
    ld = [None] * NCH
    ldm = [None, None]
    xf_done = [None, None]
    rd_bcast = None
    ev_router = None
    ev_qk = None
    rd_tr = None
    st_xnb = None
    st_nmb = None
    prev_rt = None
    prev_qmm = None
    mrf = [(f"aSTm{i}", K.cnt[f"aSTm{i}"]) for i in range(2)]
    ld[0] = K.dma("sync", xf[0][:], xT_pct[:, :, 0:TC], "aXF0")
    K.dma("sync", mr3[0][0:1, 0:TC], d["s_mr"][0:1, 0:TC], "aMR0", wait=mrf)
    K.dma("sync", mr3[0][0:1, TC:2 * TC], d["s_mr"][1:2, 0:TC], "aMR0")
    ldm[0] = K.dma("sync", mr3[0][0:1, 2 * TC:3 * TC], d["s_mr"][2:3, 0:TC], "aMR0")
    for c in range(NCH):
        cb = c % 2
        tsl = slice(c * TC, (c + 1) * TC)
        if c + 1 < NCH:
            nsl = slice((c + 1) * TC, (c + 2) * TC)
            nb_ = (c + 1) % 2
            ld[c + 1] = K.dma("sync", xf[nb_][:], xT_pct[:, :, nsl],
                              f"aXF{nb_}", wait=[xf_done[nb_]])
            K.dma("sync", mr3[nb_][0:1, 0:TC], d["s_mr"][0:1, nsl],
                  f"aMR{nb_}", wait=mrf + [xf_done[nb_]])
            K.dma("sync", mr3[nb_][0:1, TC:2 * TC], d["s_mr"][1:2, nsl], f"aMR{nb_}")
            ldm[nb_] = K.dma("sync", mr3[nb_][0:1, 2 * TC:3 * TC], d["s_mr"][2:3, nsl],
                             f"aMR{nb_}")
        K.mm(psb[:, 0:512], onesf[:], mr3[cb][0:1, 0:TC], True, True,
             wait=[ldm[cb], rd_bcast])
        K.mm(psb[:, 512:1024], onesf[:], mr3[cb][0:1, TC:2 * TC], True, True)
        bc = K.mm(psb[:, 1024:1536], onesf[:], mr3[cb][0:1, 2 * TC:3 * TC],
                  True, True, inc=True)
        mb = psb[:, 0:512][:, None, :].broadcast_to((128, CC, 512))
        rb = psb[:, 512:1024][:, None, :].broadcast_to((128, CC, 512))
        r2b = psb[:, 1024:1536][:, None, :].broadcast_to((128, CC, 512))
        sub_pt = K.op("vector", "tensor_sub", wait=[bc, ld[c], prev_rt], inc=True,
                      out=xnf[:], in0=xf[cb][:], in1=mb)
        xf_done[cb] = sub_pt
        xnpt = K.op("vector", "tensor_mul", inc=True, out=xnf[:], in0=xnf[:], in1=rb)
        xnbpt = K.op("vector", "tensor_copy", wait=[st_xnb], inc=True,
                     out=xnb[:], in_=xnf[:])
        st_xnb = K.dma("sync", d["s_xnb"][:].rearrange("c p t -> p c t")[:, :, tsl],
                       xnb[:], "aSTx", wait=[xnbpt])
        rt = None
        for i in range(CC):
            rt = K.mm(pssc[:], gwT[:, i, :], xnf[:, i, :], start=(i == 0),
                      stop=(i == CC - 1), wait=[xnpt, ev_router], inc=(i == CC - 1))
        prev_rt = rt
        ev_router = K.act(scraw[:, tsl], pssc[:], AF.Identity, wait=[rt], inc=True)
        rd_bcast = K.op("vector", "tensor_mul", wait=[xnpt], inc=True,
                        out=nmp[:], in0=xnf[:], in1=r2b)
        npt = None
        for i in range(CC):
            npt = K.op("vector", "tensor_scalar", inc=(i == CC - 1),
                       wait=([st_nmb, prev_qmm] if i == 0 else ()),
                       out=nmb[:, i, :], in0=nmp[:, i, :],
                       scalar1=lng[:, i:i + 1], scalar2=lnb[:, i:i + 1],
                       op0=ALU.mult, op1=ALU.add, const_scalars=True)
        st_nmb = K.dma("sync", d["s_nmb"][:].rearrange("c p t -> p c t")[:, :, tsl],
                       nmb[:], "aSTn", wait=[npt])
        qmm = None
        for i in range(CC):
            qmm = K.mm(psqk[:], wqk[:, i, 0, :], nmb[:, i, :], start=(i == 0),
                       stop=(i == CC - 1), wait=[npt, wqk_pt, ev_qk],
                       inc=(i == CC - 1))
        prev_qmm = qmm
        qk_pt = K.act(qk16[:], psqk[:], AF.Silu, bias=bqk[:], wait=[qmm], inc=True)
        ev_qk = qk_pt
        K.op("vector", "tensor_scalar", wait=[qk_pt], out=qqT[:, tsl], in0=qk16[:],
             scalar1=qkg[:, 0:1], scalar2=qkb[:, 0:1], op0=ALU.mult, op1=ALU.add, const_scalars=True)
        K.op("vector", "tensor_scalar", out=lqT[:, tsl], in0=qk16[:],
             scalar1=qkg[:, 1:2], scalar2=qkb[:, 1:2], op0=ALU.mult, op1=ALU.add, const_scalars=True)
        K.op("vector", "tensor_scalar", out=qkT2[:, tsl], in0=qk16[:],
             scalar1=qkg[:, 2:3], scalar2=qkb[:, 2:3], op0=ALU.mult, op1=ALU.add, const_scalars=True)
        lk_pt = K.op("vector", "tensor_scalar", inc=True, out=lkT[:], in0=qk16[:],
                     scalar1=qkg[:, 3:4], scalar2=qkb[:, 3:4], op0=ALU.mult, op1=ALU.add, const_scalars=True)
        for j in range(4):
            tp = K.tr(pstr[:], lkT[:, j * 128:(j + 1) * 128], idf[:],
                      wait=[lk_pt, rd_tr], inc=True)
            rd_tr = K.op("vector", "tensor_copy", wait=[tp], inc=True,
                         out=ltok[:, (c * 4 + j) * 128:(c * 4 + j + 1) * 128],
                         in_=pstr[:])
        if DEV:
            p1 = K.dma("sync", d["dbg_xn"][:].rearrange("c p t -> p c t")[:, :, tsl],
                       xnf[:], "zDBG", wait=[xnpt])
            p2 = K.dma("gpsimd", d["dbg_nmb"][:].rearrange("c p t -> p c t")[:, :, tsl],
                       nmb[:], "zDBG2", wait=[npt])
            p3 = K.dma("gpsimd", d["dbg_qk"][:, tsl], qk16[:], "zDBG2", wait=[qk_pt])
            K.barrier()

    K.barrier()
    scs = K.sbuf("A_scs", (16, T), F32)
    sg_pt = K.act(scs[:], scraw[:], AF.Sigmoid, inc=True)
    stok = K.sbuf("A_stok", (128, 16), F32)
    m1t = K.sbuf("A_m1t", (128, 1), F32)
    m2t = K.sbuf("A_m2t", (128, 1), F32)
    msk = K.sbuf("A_msk", (128, 16), F32)
    mski = K.sbuf("A_mski", (128, 16), I32)
    m1b = K.sbuf("A_m1b", (128, 16), F32)
    rtok = K.sbuf("A_rtok", (128, 16), F32)
    wqt = K.sbuf("A_wqt", (128, 1), F32)
    ones16 = K.sbuf("A_ones16", (128, 16), F32)
    o16 = K.op("vector", "memset", ones16[:], 1.0, inc=True)
    rd1 = None
    rd2 = None
    for tt in range(T // 128):
        csl = slice(tt * 128, (tt + 1) * 128)
        tp1 = K.tr(pstr[:, 0:16], scs[:, csl], idf[0:16, 0:16], wait=[sg_pt, rd1], inc=True)
        p = K.op("vector", "tensor_copy", wait=[tp1, o16], inc=True,
                 out=stok[:], in_=pstr[:, 0:16])
        rd1 = p
        p = K.op("vector", "reduce_max", swait=[p], inc=True,
                 out=m1t[:], in_=stok[:], axis=AX.X)
        p = K.op("vector", "tensor_scalar", swait=[p], inc=True, out=m1b[:],
                 in0=ones16[:], scalar1=m1t[:], scalar2=0.0, op0=ALU.mult, op1=ALU.add)
        p = K.op("vector", "tensor_tensor", swait=[p], inc=True, out=mski[:],
                 in0=stok[:], in1=m1b[:], op=ALU.is_ge)
        p = K.op("vector", "tensor_copy", swait=[p], inc=True, out=msk[:], in_=mski[:])
        p = K.op("vector", "tensor_scalar", swait=[p], inc=True, out=msk[:],
                 in0=msk[:], scalar1=1e30, scalar2=None, op0=ALU.mult)
        p = K.op("vector", "tensor_sub", swait=[p], inc=True, out=msk[:],
                 in0=stok[:], in1=msk[:])
        p = K.op("vector", "reduce_max", swait=[p], inc=True,
                 out=m2t[:], in_=msk[:], axis=AX.X)
        p = K.op("vector", "tensor_add", swait=[p], inc=True, out=wqt[:],
                 in0=m1t[:], in1=m2t[:])
        p = K.op("vector", "tensor_scalar", swait=[p], inc=True, out=wqt[:],
                 in0=wqt[:], scalar1=1e-20, scalar2=None, op0=ALU.add)
        p = K.op("vector", "reciprocal", swait=[p], inc=True, out=wqt[:], in_=wqt[:])
        p = K.op("vector", "tensor_scalar", swait=[p], inc=True, out=wqt[:],
                 in0=wqt[:], scalar1=MOE_SCALE, scalar2=None, op0=ALU.mult)
        p = K.op("vector", "tensor_scalar", swait=[p], inc=True, out=rtok[:],
                 in0=ones16[:], scalar1=m2t[:], scalar2=0.0, op0=ALU.mult, op1=ALU.add)
        p = K.op("vector", "tensor_tensor", swait=[p], inc=True, out=mski[:],
                 in0=stok[:], in1=rtok[:], op=ALU.is_ge)
        p = K.op("vector", "tensor_copy", swait=[p], inc=True, out=rtok[:], in_=mski[:])
        p = K.op("vector", "tensor_mul", swait=[p], inc=True, out=rtok[:],
                 in0=rtok[:], in1=stok[:])
        rt2 = K.op("vector", "tensor_scalar", swait=[p], inc=True, out=rtok[:],
                   in0=rtok[:], scalar1=wqt[:], scalar2=None, op0=ALU.mult)
        tp2 = K.tr(psqk[0:16, 0:128], rtok[:], idf[:], wait=[rt2, rd2], inc=True)
        rd2 = K.op("vector", "tensor_copy", wait=[tp2], swait=[rt2], inc=True,
                   out=routeT[:, csl], in_=psqk[0:16, 0:128])
        if DEV and tt == 0:
            K.dma("sync", d["dbg_stok"][:], stok[:], "zDBG", wait=[rd1])
            K.dma("sync", d["dbg_rtok"][:], rtok[:], "zDBG", wait=[rt2])
            K.barrier()
    K.dma("sync", d["s_route"][:], routeT[:], "aSTr", wait=[rd2])
    if DEV:
        K.dma("sync", d["dbg_route"][:], routeT[:], "zDBG", wait=[rd2])
        K.dma("sync", d["dbg_scs"][:], scs[:], "zDBG", wait=[sg_pt])
    K.barrier()
    K.free("A_scs", "A_stok", "A_m1t", "A_m2t", "A_msk", "A_mski", "A_m1b",
           "A_rtok", "A_wqt", "A_ones16", "A_scraw", "routeT",
           "A_xf0", "A_xf1", "A_xnf", "A_nmp", "A_nmb", "A_xnb", "A_qk16",
           "A_lkT", "A_mr30", "A_mr31",
           "A_psb", "A_pssc", "A_psqk", "A_pstr")

def phase_L0(K, d, wp, st):
    idf = K.store["c_idf"][1]
    onesf = K.store["c_onesf"][1]
    qqT = K.store["qqT"][1]
    qkT2 = K.store["qkT2"][1]
    lqT = K.store["lqT"][1]
    ltok = K.store["ltok"][1]

    bhg = K.sbuf("L_bhg", (128, 16), F32)
    bo = K.sbuf("L_bo", (128, 8), F32)
    bhv = K.sbuf("L_bhv", (1, HID), BF16)
    onesrb = K.sbuf("L_onesrb", (1, 128), BF16)
    braw = K.sbuf("L_braw", (128, 2, 256), F32)
    bsc = K.sbuf("L_bsc", (128, 512), F32)
    linkv = K.sbuf("L_linkv", (128, HID), BF16)
    K.dma("sync", bhg[:], d["bhg"][:], "lCST")
    K.dma("sync", bo[:], d["bo"][:], "lCST")
    K.dma("sync", bhv[:], d["s_bhv"][:], "lCST", wait=[wp["bhv"]])
    lcst_pt = K.dma("sync", braw[:], d["biasT"][:], "lCST")
    bo_pt = lcst_pt
    bhv_pt = lcst_pt
    braw_pt = lcst_pt
    ob_pt = K.op("vector", "memset", onesrb[:], 1.0, inc=True)
    bsc_pt = K.op("scalar", "mul", wait=[braw_pt], inc=True,
                  out=bsc[:].rearrange("p (j i) -> p j i", j=2), in_=braw[:],
                  mul=float(GSZ * math.sqrt(QKD)))

    PP = K.psum("L_PP", (128, 2048), F32)
    PQ = K.psum("L_PQ", (128, 2048), F32)

    nmb_pct = d["s_nmb"][:].rearrange("c p t -> p c t")
    xT_pct = d["xT"][:].rearrange("c p t -> p c t")
    x1_pct = d["s_x1"][:].rearrange("c p t -> p c t")

    for b in range(NB):
        whv = K.sbuf(f"L_whv{b}", (128, CC, HID), BF16)
        nmch = [K.sbuf(f"L_nmch{b}_{i}", (128, CC, TC), BF16) for i in range(2)]
        vt = [K.sbuf(f"L_vt{b}_{i}", (128, HID), BF16) for i in range(2)]
        whv_pt = K.dma("sync", whv[:], d["s_whv"][:].rearrange("c p t -> p c t"),
                       "lWHV", wait=[wp["whv"]])
        ev_v = [None, None]
        vt_used = [None, None]
        ld_nm = [None, None]
        nm_used = [None, None]
        kvmm = None
        for ch in range(4):
            cb = ch % 2
            gtsl = slice(b * S + ch * TC, b * S + (ch + 1) * TC)
            ld_nm[cb] = K.dma("sync", nmch[cb][:], nmb_pct[:, :, gtsl],
                              f"lNM{cb}", wait=[nm_used[cb], whv_pt, bhv_pt,
                                                ("aSTn", K.cnt["aSTn"])])
            for tt in range(4):
                t_loc = ch * 4 + tt
                tb = t_loc % 2
                for fc in range(4):
                    par = fc % 2
                    reg = PQ[:, par * 512:(par + 1) * 512]
                    mmv = None
                    for i in range(CC):
                        mmv = K.mm(reg, nmch[cb][:, i, tt * 128:(tt + 1) * 128],
                                   whv[:, i, fc * 512:(fc + 1) * 512],
                                   start=(i == 0), stop=False,
                                   wait=(ld_nm[cb], ev_v[par], vt_used[tb]))
                    mmv = K.mm(reg, onesrb[:], bhv[0:1, fc * 512:(fc + 1) * 512],
                               start=False, stop=True, wait=[ob_pt], inc=True)
                    ev_v[par] = K.act(vt[tb][:, fc * 512:(fc + 1) * 512], reg,
                                      AF.Silu, wait=[mmv, vt_used[tb]], inc=True)
                gt = b * 16 + t_loc
                for ec in range(4):
                    kvmm = K.mm(PP[:, ec * 512:(ec + 1) * 512],
                                ltok[:, gt * 128:(gt + 1) * 128],
                                vt[tb][:, ec * 512:(ec + 1) * 512],
                                start=(t_loc == 0), stop=(t_loc == 15),
                                wait=[ev_v[0], ev_v[1]],
                                inc=(t_loc == 15 and ec == 3))
                stv = K.dma("sync", d["s_vt"][b, t_loc], vt[tb][:],
                            f"lVT{b}_{tb}", wait=[ev_v[0], ev_v[1]])
                vt_used[tb] = stv
            nm_used[cb] = ("TE", K.cnt["TE"])
        kv_pt = K.act(linkv[:], PP[:, 0:2048], AF.Identity, scale=1.0 / S,
                      wait=[kvmm], inc=True)
        K.barrier()
        K.free(f"L_whv{b}", f"L_nmch{b}_0", f"L_nmch{b}_1",
               f"L_vt{b}_0", f"L_vt{b}_1")

        whg = K.sbuf(f"L_whg{b}", (128, CC, 16, 128), BF16)
        wo = K.sbuf(f"L_wo{b}", (128, 16, CC, 128), BF16)
        gate16 = K.sbuf(f"L_gate16{b}", (128, 16, 256), BF16)
        pre16 = K.sbuf(f"L_pre16{b}", (128, 16, 256), BF16)
        attnT = K.sbuf(f"L_attnT{b}", (128, 512), BF16)
        afp = K.sbuf(f"L_afp{b}", (128, 512), F32)
        nmg = [K.sbuf(f"L_nmg{b}_{i}", (128, CC, 256), BF16) for i in range(2)]
        xfg = [K.sbuf(f"L_xfg{b}_{i}", (128, CC, 256), F32) for i in range(2)]
        mrow = [K.sbuf(f"L_mrow{b}_{i}", (1, 512), F32) for i in range(2)]
        vg = [K.sbuf(f"L_vg{b}_{i}", (128, 2, HID), BF16) for i in range(2)]
        xng = K.sbuf(f"L_xng{b}", (128, CC, 256), F32)
        x1g = K.sbuf(f"L_x1g{b}", (128, CC, 256), F32)
        K.dma("sync", whg[:], d["s_whg"][:], "lWGO", wait=[wp["whg"]])
        wgo_pt = K.dma("sync", wo[:], d["s_wo"][:], "lWGO", wait=[wp["wo"]])
        whg_pt = wgo_pt
        wo_pt = wgo_pt

        ld_g = [None, None]
        g_used = [None, None]
        ev_sim = None
        ev_gate = None
        at_pt = None
        ev_ql = None
        ev_wo = None
        ev_bc = None
        st_x1 = None
        for gi in range(8):
            gb = gi % 2
            gg = b * 8 + gi
            gcols = slice(gg * 256, (gg + 1) * 256)
            if gi == 0:
                astf = [("aSTn", K.cnt["aSTn"]), ("aSTm0", K.cnt["aSTm0"]),
                        ("aSTm1", K.cnt["aSTm1"])]
                K.dma("sync", nmg[0][:], nmb_pct[:, :, gcols], f"lGR0_{b}", wait=astf)
                K.dma("sync", xfg[0][:], xT_pct[:, :, gcols], f"lGR0_{b}")
                K.dma("sync", mrow[0][0:1, 0:256], d["s_mr"][0:1, gcols], f"lGR0_{b}")
                g0pt = K.dma("sync", mrow[0][0:1, 256:512], d["s_mr"][1:2, gcols], f"lGR0_{b}")
                v0pt = K.dma("sync", vg[0][:],
                             d["s_vt"][b, 0:2].rearrange("j p f -> p j f"),
                             f"lVG0_{b}",
                             wait=[(f"lVT{b}_0", K.cnt[f"lVT{b}_0"]),
                                   (f"lVT{b}_1", K.cnt[f"lVT{b}_1"])])
                ld_g[0] = [g0pt, g0pt, g0pt, g0pt, v0pt]
            if gi + 1 < 8:
                nx = slice((gg + 1) * 256, (gg + 2) * 256)
                nb_ = (gi + 1) % 2
                w_ = [g_used[nb_]]
                K.dma("sync", nmg[nb_][:], nmb_pct[:, :, nx], f"lGR{nb_}_{b}", wait=w_)
                K.dma("sync", xfg[nb_][:], xT_pct[:, :, nx], f"lGR{nb_}_{b}")
                K.dma("sync", mrow[nb_][0:1, 0:256], d["s_mr"][0:1, nx], f"lGR{nb_}_{b}")
                gpt = K.dma("sync", mrow[nb_][0:1, 256:512], d["s_mr"][1:2, nx], f"lGR{nb_}_{b}")
                vpt = K.dma("sync", vg[nb_][:],
                            d["s_vt"][b, 2 * (gi + 1):2 * (gi + 1) + 2].rearrange("j p f -> p j f"),
                            f"lVG{nb_}_{b}", wait=w_)
                ld_g[nb_] = [gpt, gpt, gpt, gpt, vpt]
            mms = None
            for j in range(2):
                mms = K.mm(PQ[:, j * 256:(j + 1) * 256],
                           qkT2[:, (gg * 2 + j) * 128:(gg * 2 + j + 1) * 128],
                           qqT[:, gcols], start=(j == 0), stop=(j == 1),
                           wait=[ev_sim], inc=(j == 1))
            ev_sim = K.op("vector", "tensor_add", wait=[mms, bsc_pt, at_pt], inc=True,
                          out=afp[:], in0=bsc[:], in1=PQ[:, 0:512])
            rl = K.act(afp[:], afp[:], AF.Relu, wait=[ev_sim], inc=True)
            at_pt = K.act(attnT[:], afp[:], AF.Square, scale=1.0 / GSZ, inc=True)
            for rnd in range(2):
                for fi in range(8):
                    ft = rnd * 8 + fi
                    reg = PP[:, fi * 256:(fi + 1) * 256]
                    for i in range(CC):
                        mmg = K.mm(reg, whg[:, i, ft, :], nmg[gb][:, i, :],
                                   start=(i == 0 and fi % 2 == 0),
                                   stop=(i == CC - 1 and fi % 2 == 1),
                                   wait=(ld_g[gb][0], whg_pt, ev_gate, ev_wo),
                                   inc=(fi == 7 and i == CC - 1))
                for fi in range(8):
                    ft = rnd * 8 + fi
                    ev_gate = K.act(gate16[:, ft, :], PP[:, fi * 256:(fi + 1) * 256],
                                    AF.Silu, bias=bhg[:, ft:ft + 1],
                                    wait=[mmg], inc=True)
            for rnd in range(4):
                mmq = None
                for ei in range(4):
                    et = rnd * 4 + ei
                    reg = PQ[:, 512 + ei * 256:512 + (ei + 1) * 256]
                    mmq = K.mm(reg, vg[gb][:, 0, et * 128:(et + 1) * 128],
                               attnT[:, 0:256], start=(ei % 2 == 0), stop=False,
                               wait=(at_pt, ev_ql, kv_pt, ld_g[gb][4]))
                    mmq = K.mm(reg, vg[gb][:, 1, et * 128:(et + 1) * 128],
                               attnT[:, 256:512], start=False, stop=False)
                    mmq = K.mm(reg, linkv[:, et * 128:(et + 1) * 128],
                               lqT[:, gcols], start=False, stop=(ei % 2 == 1),
                               inc=(ei == 3))
                ev_ql = K.op("vector", "tensor_mul", wait=[mmq, ev_gate], inc=True,
                             out=pre16[:, rnd * 4:(rnd + 1) * 4, :],
                             in0=gate16[:, rnd * 4:(rnd + 1) * 4, :],
                             in1=PQ[:, 512:1536].rearrange("p (e i) -> p e i", e=4))
            mmw = None
            for dt in range(8):
                reg = PP[:, (dt // 2) * 512 + (dt % 2) * 256:
                          (dt // 2) * 512 + (dt % 2 + 1) * 256]
                for ec in range(16):
                    mmw = K.mm(reg, wo[:, ec, dt, :], pre16[:, ec, :],
                               start=(ec == 0 and dt % 2 == 0),
                               stop=(ec == 15 and dt % 2 == 1),
                               wait=(ev_ql, wo_pt, ev_wo, ev_gate),
                               inc=(dt == 7 and ec == 15))
            mmb = K.mm(PQ[:, 1536:1792], onesf[:], mrow[gb][0:1, 0:256], True, False,
                       wait=[ld_g[gb][0], ev_bc])
            mmb = K.mm(PQ[:, 1792:2048], onesf[:], mrow[gb][0:1, 256:512], False, True,
                       inc=True)
            oo = None
            for dt in range(8):
                reg = PP[:, (dt // 2) * 512 + (dt % 2) * 256:
                          (dt // 2) * 512 + (dt % 2 + 1) * 256]
                oo = K.act(x1g[:, dt, :], reg, AF.Identity, bias=bo[:, dt:dt + 1],
                           wait=[mmw, bo_pt, st_x1], inc=True)
            ev_wo = oo
            mbv = PQ[:, 1536:1792][:, None, :].broadcast_to((128, CC, 256))
            rbv = PQ[:, 1792:2048][:, None, :].broadcast_to((128, CC, 256))
            K.op("vector", "tensor_sub", wait=[mmb, ld_g[gb][1]],
                 out=xng[:], in0=xfg[gb][:], in1=mbv)
            ev_bc = K.op("vector", "tensor_mul", inc=True,
                         out=xng[:], in0=xng[:], in1=rbv)
            K.op("vector", "tensor_add", wait=[oo], out=x1g[:], in0=x1g[:], in1=xng[:])
            gu = K.op("vector", "tensor_add", inc=True,
                      out=x1g[:], in0=x1g[:], in1=xfg[gb][:])
            g_used[gb] = gu
            st_x1 = K.dma("sync", x1_pct[:, :, gcols], x1g[:], "lSX", wait=[gu])
        K.barrier()
        K.free(f"L_whg{b}", f"L_wo{b}", f"L_gate16{b}", f"L_pre16{b}",
               f"L_attnT{b}", f"L_afp{b}", f"L_nmg{b}_0", f"L_nmg{b}_1",
               f"L_xfg{b}_0", f"L_xfg{b}_1", f"L_mrow{b}_0", f"L_mrow{b}_1",
               f"L_vg{b}_0", f"L_vg{b}_1", f"L_xng{b}", f"L_x1g{b}")
    if DEV:
        K.dma("sync", d["dbg_x1"][:], d["s_x1"][:], "zDBG",
              wait=[("lSX", K.cnt["lSX"])])
    K.barrier()
    K.free("L_bhg", "L_bo", "L_bhv", "L_onesrb", "L_braw", "L_bsc", "L_linkv",
           "L_PP", "L_PQ", "qqT", "qkT2", "lqT", "ltok")
    return {}

def phase_B(K, d, wp, st):
    onesf = K.store["c_onesf"][1]

    acc = K.sbuf("B_acc", (128, CC, S), F32)
    xnh = K.sbuf("B_xnh", (128, CC, S), BF16)
    wg = [K.sbuf(f"B_wg{i}", (128, 4, CC, 128), BF16) for i in range(2)]
    wu = [K.sbuf(f"B_wu{i}", (128, 4, CC, 128), BF16) for i in range(2)]
    wd = [K.sbuf(f"B_wd{i}", (128, CC, 4, 128), BF16) for i in range(2)]
    rrow = [K.sbuf(f"B_rrow{i}", (1, S), F32) for i in range(2)]
    sg16 = K.sbuf("B_sg16", (128, 4, 512), BF16)
    sb16 = K.sbuf("B_sb16", (128, CC, 512), BF16)
    xnb1 = K.sbuf("B_xnb1", (128, CC, 512), BF16)
    xtm = K.sbuf("B_xtm", (128, CC, 512), BF16)
    brow = K.sbuf("B_brow", (1, 1024), F32)
    brow2 = K.sbuf("B_brow2", (1, 1024), F32)
    ceps2 = K.store["c_eps"][1]
    onesb = K.store["c_onesb"][1]
    hr16 = K.sbuf("B_hr16", (128, 4, 512), BF16)
    PG = K.psum("B_PG", (128, 2048), F32)
    PU = K.psum("B_PU", (128, 2048), F32)

    x1_pct = d["s_x1"][:].rearrange("c p t -> p c t")
    xnb_pct = d["s_xnb"][:].rearrange("c p t -> p c t")

    NEXP = NE + 1
    wld = [None, None]
    wused = [None, None]
    st_acc = None
    for half in range(NB):
        hsl = slice(half * S, (half + 1) * S)
        K.dma("sync", acc[:], x1_pct[:, :, hsl], f"bAC{half}",
              wait=[st_acc, ("lSX", K.cnt["lSX"])])
        ld_pt = K.dma("sync", xnh[:], xnb_pct[:, :, hsl], f"bAC{half}",
                      wait=[("TE", K.cnt["TE"]), ("aSTx", K.cnt["aSTx"])])
        ld_acc = ld_pt
        ld_xnh = ld_pt
        evA = None
        evB = None
        evAt = [None] * 4
        evBt = [None] * 4
        ev_sg = None
        sgr_prev = None
        hr_pt = None
        first = (half == 0)
        for e in range(NEXP):
            eb = e % 2
            if half == 0 or e >= 2:
                pass
            srcg = d["s_mwg"][e] if e < NE else d["s_wsg"][:]
            srcu = d["s_mwu"][e] if e < NE else d["s_wsu"][:]
            srcd = d["s_mwd"][e] if e < NE else d["s_wsd"][:]
            wprep = wp[f"moe{e}"] if e < NE else wp["moeS"]
            K.dma("sync", wg[eb][:], srcg, f"bWL{eb}", wait=[wused[eb], wprep])
            K.dma("sync", wu[eb][:], srcu, f"bWL{eb}", wait=[wprep])
            l3 = K.dma("sync", wd[eb][:], srcd, f"bWL{eb}", wait=[wprep])
            if e < NE:
                l3 = K.dma("sync", rrow[eb][0:1, :], d["s_route"][e:e + 1, hsl],
                           f"bWL{eb}", wait=[("aSTr", K.cnt["aSTr"])])
            wld[eb] = l3
            lr = l3
            for ch in range(4):
                csl = slice(ch * 512, (ch + 1) * 512)
                gmm = None
                for mt in range(4):
                    for i in range(CC):
                        gmm = K.mm(PG[:, mt * 512:(mt + 1) * 512],
                                   wg[eb][:, mt, i, :], xnh[:, i, csl],
                                   start=(i == 0), stop=(i == CC - 1),
                                   wait=(ld_xnh, wld[eb], evBt[mt]),
                                   inc=(mt == 3 and i == CC - 1))
                ev_sg = K.act(sg16[:], PG[:, 0:2048].rearrange("p (m t) -> p m t", m=4),
                              AF.Silu, wait=[gmm], inc=True)
                if e < NE:
                    rmm = K.mm(PG[:, 0:512], onesf[:], rrow[eb][0:1, csl],
                               True, True, wait=[ev_sg, lr], inc=True)
                    sgr_pt = K.op("vector", "tensor_mul", wait=[rmm], inc=True,
                                  out=sg16[:], in0=sg16[:],
                                  in1=PG[:, 0:512][:, None, :].broadcast_to((128, 4, 512)))
                else:
                    sgr_pt = ev_sg
                ummt = [None] * 4
                for mt in range(4):
                    for i in range(CC):
                        ummt[mt] = K.mm(PU[:, mt * 512:(mt + 1) * 512],
                                        wu[eb][:, mt, i, :], xnh[:, i, csl],
                                        start=(i == 0), stop=(i == CC - 1),
                                        wait=(evAt[mt],), inc=(i == CC - 1))
                hr_pt = None
                for mt in range(4):
                    hr_pt = K.op("vector", "tensor_mul", wait=[ummt[mt]],
                                 swait=[sgr_pt], inc=True,
                                 out=hr16[:, mt, :], in0=sg16[:, mt, :],
                                 in1=PU[:, mt * 512:(mt + 1) * 512])
                hsrc = hr16
                for dt in range(4):
                    dA = None
                    for mc in range(4):
                        dA = K.mm(PU[:, dt * 512:(dt + 1) * 512],
                                  wd[eb][:, dt, mc, :], hsrc[:, mc, :],
                                  start=(mc == 0), stop=(mc == 3),
                                  wait=(hr_pt,), inc=(mc == 3))
                    evAt[dt] = K.op("vector", "tensor_add", wait=[dA, ld_acc],
                                    swait=[hr_pt], inc=True,
                                    out=acc[:, dt, csl], in0=acc[:, dt, csl],
                                    in1=PU[:, dt * 512:(dt + 1) * 512])
                evA = evAt[3]
                for dt in range(4):
                    dB = None
                    for mc in range(4):
                        dB = K.mm(PG[:, dt * 512:(dt + 1) * 512],
                                  wd[eb][:, 4 + dt, mc, :], hsrc[:, mc, :],
                                  start=(mc == 0), stop=(mc == 3),
                                  wait=(hr_pt, sgr_pt), inc=(mc == 3))
                    evBt[dt] = K.op("vector", "tensor_add", wait=[dB], inc=True,
                                    out=acc[:, 4 + dt, csl], in0=acc[:, 4 + dt, csl],
                                    in1=PG[:, dt * 512:(dt + 1) * 512])
                evB = evBt[3]
                sgr_prev = sgr_pt
            wused[eb] = ("TE", K.cnt["TE"])
        st_acc = K.dma("sync", x1_pct[:, :, hsl], acc[:], "bSX", wait=[evA, evB])
        ev2 = None
        xn1pt = None
        for ch in range(4):
            csl = slice(ch * 512, (ch + 1) * 512)
            tsl2 = slice(half * S + ch * 512, half * S + (ch + 1) * 512)
            cpt = K.op("vector", "tensor_copy", wait=[evA, evB, xn1pt], inc=True,
                       out=sb16[:], in_=acc[:, :, csl])
            mmx = None
            for i in range(CC):
                mmx = K.mm(PG[0:1, 0:512], onesb[:], sb16[:, i, :], start=(i == 0),
                           stop=(i == CC - 1), wait=[cpt, ev2], inc=(i == CC - 1))
            sqt = K.act(sb16[:], acc[:, :, csl], AF.Square, wait=[evA, evB, mmx],
                        inc=True)
            mmst = None
            for i in range(CC):
                mmst = K.mm(PG[0:1, 512:1024], onesb[:], sb16[:, i, :], start=(i == 0),
                            stop=(i == CC - 1), wait=[sqt], inc=(i == CC - 1))
            K.act(brow[0:1, 0:512], PG[0:1, 0:512], AF.Identity, scale=1.0 / D,
                  wait=[mmst])
            ev2 = K.act(brow[0:1, 512:1024], PG[0:1, 512:1024], AF.Identity,
                        scale=1.0 / D, inc=True)
            K.dma("sync", d["s_mr1"][0:1, tsl2], brow[0:1, 0:512], "cSTm", wait=[ev2])
            K.op("vector", "tensor_mul", wait=[ev2], out=brow2[0:1, 0:512],
                 in0=brow[0:1, 0:512], in1=brow[0:1, 0:512])
            vv = K.op("vector", "tensor_sub", inc=True, out=brow2[0:1, 0:512],
                      in0=brow[0:1, 512:1024], in1=brow2[0:1, 0:512])
            sdp = K.act(brow2[0:1, 512:1024], brow2[0:1, 0:512], AF.Sqrt,
                        bias=ceps2[0:1, :], wait=[vv], inc=True)
            rrp = K.op("vector", "reciprocal", wait=[sdp], inc=True,
                       out=brow2[0:1, 0:512], in_=brow2[0:1, 512:1024])
            K.dma("sync", d["s_mr1"][1:2, tsl2], brow2[0:1, 0:512], "cSTm", wait=[rrp])
            K.mm(PU[:, 0:512], onesf[:], brow[0:1, 0:512], True, True,
                 wait=[rrp, xn1pt])
            bcp = K.mm(PU[:, 512:1024], onesf[:], brow2[0:1, 0:512], True, True,
                       inc=True)
            mbv2 = PU[:, 0:512][:, None, :].broadcast_to((128, CC, 512))
            rbv2 = PU[:, 512:1024][:, None, :].broadcast_to((128, CC, 512))
            K.op("vector", "tensor_sub", wait=[bcp], out=xtm[:],
                 in0=acc[:, :, csl], in1=mbv2)
            xn1pt = K.op("vector", "tensor_mul", inc=True, out=xnb1[:],
                         in0=xtm[:], in1=rbv2)
            K.dma("sync", d["s_xn1b"][:].rearrange("c p t -> p c t")[:, :, tsl2],
                  xnb1[:], "cSTx", wait=[xn1pt])
    if DEV:
        K.dma("sync", d["dbg_x1f"][:], d["s_x1"][:], "zDBG",
              wait=[("bSX", K.cnt["bSX"])])
    K.barrier()
    K.free("B_acc", "B_xnh", "B_wg0", "B_wg1", "B_wu0", "B_wu1", "B_wd0",
           "B_wd1", "B_rrow0", "B_rrow1", "B_sg16", "B_sb16",
           "B_xnb1", "B_xtm", "B_brow", "B_brow2",
           "B_hr16", "B_PG", "B_PU")
    return {}

def phase_C(K, d, wp, st):
    onesf = K.store["c_onesf"][1]
    onesb = K.store["c_onesb"][1]
    idf = K.store["c_idf"][1]
    ceps = K.store["c_eps"][1]

    x1_pct = d["s_x1"][:].rearrange("c p t -> p c t")
    xn1_pct = d["s_xn1b"][:].rearrange("c p t -> p c t")
    aout_pct = d["s_aout"][:].rearrange("c p t -> p c t")
    out_pct = d["out"][:].rearrange("c p t -> p c t")

    bsx = ("bSX", K.cnt["bSX"])

    mbuf = K.sbuf("C_mbuf", (128, 16), F32)
    m1t = K.sbuf("C_m1t", (128, 1), F32)
    cm44 = K.sbuf("C_cm44", (1, 1), F32)
    K.op("vector", "memset", cm44[:], -44.0)
    mrow1 = K.sbuf("C_mrow1", (1, 128), F32)
    negc = K.sbuf("C_negc", (1, 1), F32)
    cbias = K.sbuf("C_cbias", (128, 1), F32)
    aob = K.sbuf("C_aob", (128, CC, 512), F32)
    for b in range(NB):
        hsl = slice(b * S, (b + 1) * S)
        q16 = K.sbuf(f"C_q16{b}", (128, CC, S), BF16)
        k16 = K.sbuf(f"C_k16{b}", (128, CC, S), BF16)
        vaug = K.sbuf(f"C_vaug{b}", (128, 16, 16, 65), BF16)
        xn1h = K.sbuf(f"C_xn1h{b}", (128, CC, S), BF16)
        wq1 = K.sbuf(f"C_wq1{b}", (128, CC, CC, 128), BF16)
        wk1 = K.sbuf(f"C_wk1{b}", (128, CC, CC, 128), BF16)
        wv1 = K.sbuf(f"C_wv1{b}", (128, CC, 1024), BF16)
        CP = K.psum(f"C_CP{b}", (128, 2048), F32)
        ldx = K.dma("sync", xn1h[:], xn1_pct[:, :, hsl], f"cXN{b}",
                    wait=[("cSTx", K.cnt["cSTx"])])
        K.dma("gpsimd", wq1[:], d["s_wq1"][:], f"cWQ{b}", wait=[wp["PREP"]])
        wqk_pt = K.dma("gpsimd", wk1[:], d["s_wk1"][:], f"cWQ{b}")
        wl_pt = K.dma("sync", wv1[:], d["s_wv1"][:].rearrange("c p f -> p c f"),
                      f"cWV{b}")
        on_pt = K.op("vector", "memset", vaug[:, :, :, 64:65], 1.0, inc=True)
        evq = [None] * 4
        ev_q = None
        mmq = None
        rix = 0
        for dst, w, lab in ((q16, wq1, 0), (k16, wk1, 1)):
            for ft in range(CC):
                for tc in range(4):
                    reg = CP[:, (rix % 4) * 512:(rix % 4 + 1) * 512]
                    for i in range(CC):
                        mmq = K.mm(reg, w[:, i, ft, :], xn1h[:, i, tc * 512:(tc + 1) * 512],
                                   start=(i == 0), stop=(i == CC - 1),
                                   wait=(ldx, wl_pt, wqk_pt, evq[rix % 4]),
                                   inc=(i == CC - 1))
                    evq[rix % 4] = K.op("vector", "tensor_copy",
                                        out=dst[:, ft, tc * 512:(tc + 1) * 512], in_=reg,
                                        wait=[mmq], inc=True)
                    rix += 1
        for tt in range(16):
            for fc in range(2):
                reg = CP[:, (rix % 4) * 512:(rix % 4 + 1) * 512]
                for i in range(CC):
                    mmq = K.mm(reg, xn1h[:, i, tt * 128:(tt + 1) * 128],
                               wv1[:, i, fc * 512:(fc + 1) * 512],
                               start=(i == 0), stop=(i == CC - 1),
                               wait=(evq[rix % 4],), inc=(i == CC - 1))
                evq[rix % 4] = K.op("vector", "tensor_copy", wait=[mmq, on_pt],
                                    out=vaug[:, tt, fc * 8:(fc + 1) * 8, 0:64],
                                    in_=reg.rearrange("p (h e) -> p h e", h=8), inc=True)
                rix += 1
        ev_q = evq[(rix - 1) % 4]
        qkv_done = ("VE", K.cnt["VE"])
        K.barrier()
        K.free(f"C_wq1{b}", f"C_wk1{b}", f"C_wv1{b}", f"C_xn1h{b}")
        wo1 = K.sbuf(f"C_wo1{b}", (128, CC, CC, 128), BF16)
        expb = K.sbuf(f"C_expb{b}", (128, 16, 512), BF16)
        avc = K.sbuf(f"C_avc{b}", (128, 4, CC, 512), BF16)
        wo1_pt = K.dma("gpsimd", wo1[:], d["s_wo1"][:], f"cWO{b}", wait=[wp["PREP"]])
        CAV = K.psum(f"C_CAV{b}", (128, 512), F32)

        kaug = [K.sbuf(f"C_kaug{b}_{i}", (128, S), BF16) for i in range(2)]
        qaug = [K.sbuf(f"C_qaug{b}_{i}", (128, S), BF16) for i in range(2)]
        kz = [K.sbuf(f"C_kz{b}_{i}", (128, 512), BF16) for i in range(2)]
        zrow = [K.sbuf(f"C_zrow{b}_{i}", (1, 512), F32) for i in range(2)]
        ivbb = [K.sbuf(f"C_ivbb{b}_{i}", (64, 512), F32) for i in range(2)]
        mtmp = [K.sbuf(f"C_mtmp{b}_{i}", (16, 128), BF16) for i in range(2)]
        mbufs = [K.sbuf(f"C_mbuf2{b}_{i}", (128, 16), F32) for i in range(2)]
        P1T = K.psum(f"C_P1T{b}", (128, 1024), F32)
        CAV2 = K.psum(f"C_CAV2{b}", (128, 512), F32)
        rd_p1 = [None, None]
        ev_mt = [None, None]
        crow_pt = [None, None]
        p1ld = [None, None]
        ev_exp_reg = [None] * 4
        rd_av_reg = [None, None]
        rec_prev = [None, None]
        ibm_prev = [None, None]
        st_ao = None
        wo_mm = None
        ev_wv = None
        aug_used = [None, None]
        tr_used = None
        km = [K.op("vector", "memset", kaug[i][64:128, :], 0.0) for i in range(2)]
        qm = [K.op("vector", "memset", qaug[i][64:128, :], 0.0) for i in range(2)]
        kzm = [K.op("vector", "memset", kz[i][64:128, :], 0.0) for i in range(2)]
        kon = [K.dma("sync", kaug[i][64:65, :], d["onesr2k"][:], f"cAG{b}_{i}",
                     wait=[km[i]]) for i in range(2)]

        def pass1_load(h):
            hb = h % 2
            ft = h // 2
            pb = 64 * (h % 2)
            ck = K.dma("sync", kaug[hb][0:64, :], k16[pb:pb + 64, ft, :],
                       f"cAG{b}_{hb}", wait=[qkv_done, aug_used[hb], qm[hb], kzm[hb]])
            ckz = K.dma("sync", kz[hb][0:64, 0:128], k16[pb:pb + 64, ft, 0:128],
                        f"cAG{b}_{hb}")
            p1ld[hb] = K.dma("sync", qaug[hb][0:64, :], q16[pb:pb + 64, ft, :],
                             f"cAG{b}_{hb}")

        def pass1_step(h, step):
            hb = h % 2
            for it in range(step * 4, step * 4 + 4):
                bk = it % 2
                mm1 = K.mm(P1T[:, bk * 512:bk * 512 + 128],
                           qaug[hb][:, it * 128:(it + 1) * 128],
                           kz[hb][:, 0:128], start=True, stop=True,
                           wait=(p1ld[hb], rd_p1[bk], tr_used,
                                 ev_mt[0], ev_mt[1]), inc=True)
                rd_p1[bk] = K.op("vector", "reduce_max", wait=[mm1, ev_mt[hb]],
                                 out=mbufs[hb][:, it:it + 1],
                                 in_=P1T[:, bk * 512:bk * 512 + 128],
                                 axis=AX.X)

        def pass1_fin(h):
            nonlocal tr_used
            hb = h % 2
            tp = K.tr(P1T[0:16, 0:128], mbufs[hb][:], idf[:],
                      wait=[rd_p1[0], rd_p1[1], ev_mt[hb]], inc=True)
            tr_used = tp
            ev_mt[hb] = K.op("scalar", "mul", wait=[tp], out=mtmp[hb][:],
                             in_=P1T[0:16, 0:128], mul=-1.0, inc=True)
            c1 = K.dma("sync", d["s_crow"][hb], mtmp[hb][:], f"cAG{b}_{hb}",
                       wait=[ev_mt[hb]])
            crow_pt[hb] = K.dma("sync", qaug[hb][64:65, 0:S],
                                d["s_crow"][hb].rearrange("a i -> (a i)")[None, :],
                                f"cAG{b}_{hb}", wait=[c1])

        def pass2_emit(h, p1h):
            nonlocal aug_used
            hb = h % 2
            ft = h // 2
            pb = 64 * (h % 2)
            for ic in range(4):
                if p1h is not None and ic < 2:
                    pass1_step(p1h, 2 * ic)
                    pass1_step(p1h, 2 * ic + 1)
                if p1h is not None and ic == 2:
                    pass1_fin(p1h)
                icp = ic % 2
                isl = slice(ic * 512, (ic + 1) * 512)
                AVr = CAV if icp == 0 else CAV2
                avm = None
                exp_pts = [None] * 16
                for p in range(8):
                    mme = None
                    for jj in range(2):
                        jt = 2 * p + jj
                        reg = CP[:, (jt % 4) * 512:(jt % 4 + 1) * 512]
                        mme = K.mm(reg,
                                   kaug[hb][:, jt * 128:(jt + 1) * 128],
                                   qaug[hb][:, isl], start=True, stop=True,
                                   wait=(crow_pt[hb], kon[hb], ev_exp_reg[p % 2]),
                                   inc=(jj == 1))
                    ev_exp_reg[p % 2] = K.act(
                        expb[:, 2 * p:2 * p + 2, :].rearrange("p j i -> p (j i)"),
                        CP[:, (p % 2) * 1024:(p % 2 + 1) * 1024],
                        AF.Exp, scale=L1SCALE, wait=[mme], inc=True)
                    exp_pts[2 * p] = exp_pts[2 * p + 1] = ev_exp_reg[p % 2]
                for jv in range(16):
                    avm = K.mm(AVr[0:65, :], vaug[:, jv, h, :], expb[:, jv, :],
                               start=(jv == 0), stop=(jv == 15),
                               wait=(exp_pts[jv], rd_av_reg[icp],
                                     rec_prev[icp]),
                               inc=(jv == 15))
                zc = K.op("scalar", "copy", wait=[avm], swait=[ibm_prev[icp]],
                          out=zrow[icp][:], in_=AVr[64:65, :])
                ibm = K.mm(AVr[64:128, :], onesf[0:1, 0:64], zrow[icp][:],
                           True, True, wait=[zc, rec_prev[icp]], inc=True)
                ibm_prev[icp] = ibm
                rec = K.op("vector", "reciprocal", wait=[ibm], inc=True,
                           out=ivbb[icp][:], in_=AVr[64:128, :])
                rec_prev[icp] = rec
                rd_av_reg[icp] = K.op("vector", "tensor_mul", wait=[rec],
                                      inc=True,
                                      out=avc[pb:pb + 64, ic, ft, :],
                                      in0=AVr[0:64, :], in1=ivbb[icp][:])
            aug_used[hb] = ("TE", K.cnt["TE"])

        pass1_load(0)
        for st in range(4):
            pass1_step(0, st)
        pass1_fin(0)
        for h in range(1, NHEAD + 1):
            if h < NHEAD:
                pass1_load(h)
            pass2_emit(h - 1, h if h < NHEAD else None)
        for ic in range(4):
            for wave in range(2):
                wo_mm = None
                for dtw in range(4):
                    dt = wave * 4 + dtw
                    reg = CP[:, dtw * 512:(dtw + 1) * 512]
                    for ec in range(CC):
                        wo_mm = K.mm(reg, wo1[:, ec, dt, :], avc[:, ic, ec, :],
                                     start=(ec == 0), stop=(ec == CC - 1),
                                     wait=(rd_av_reg[0], rd_av_reg[1], wo1_pt,
                                           ev_exp_reg[0], ev_exp_reg[1],
                                           ev_exp_reg[2], ev_exp_reg[3],
                                           rd_p1[0], rd_p1[1], ev_wv),
                                     inc=(dtw == 3 and ec == CC - 1))
                for dtw in range(4):
                    dt = wave * 4 + dtw
                    ev_wv = K.act(aob[:, dt, :], CP[:, dtw * 512:(dtw + 1) * 512],
                                  AF.Identity, wait=[wo_mm, st_ao], inc=True)
            st_ao = K.dma("sync", aout_pct[:, :, b * S + ic * 512:b * S + (ic + 1) * 512],
                          aob[:], f"cAOE{b}", wait=[ev_wv])
        K.barrier()
        K.free(f"C_q16{b}", f"C_k16{b}", f"C_vaug{b}", f"C_CP{b}",
               f"C_wo1{b}", f"C_expb{b}", f"C_avc{b}", f"C_CAV{b}",
               f"C_kaug{b}_0", f"C_kaug{b}_1", f"C_qaug{b}_0", f"C_qaug{b}_1",
               f"C_kz{b}_0", f"C_kz{b}_1", f"C_zrow{b}_0", f"C_zrow{b}_1",
               f"C_ivbb{b}_0", f"C_ivbb{b}_1",
               f"C_mtmp{b}_0", f"C_mtmp{b}_1", f"C_mbuf2{b}_0", f"C_mbuf2{b}_1",
               f"C_P1T{b}", f"C_CAV2{b}")
    K.free("C_mbuf", "C_m1t", "C_cm44", "C_mrow1", "C_negc", "C_cbias",
           "C_aob")
    K.barrier()

    gam1 = K.sbuf("C2_gam1", (128, 8), F32)
    bet1 = K.sbuf("C2_bet1", (128, 8), F32)
    x1c = [K.sbuf(f"C2_x1c{i}", (128, CC, TC), F32) for i in range(2)]
    aoc = [K.sbuf(f"C2_aoc{i}", (128, CC, TC), F32) for i in range(2)]
    xnc = [K.sbuf(f"C2_xnc{i}", (128, CC, TC), BF16) for i in range(2)]
    mrw = [K.sbuf(f"C2_mrw{i}", (1, 2 * TC), F32) for i in range(2)]
    fgw = [K.sbuf(f"C2_fgw{i}", (128, CC, 128), BF16) for i in range(4)]
    fuw = [K.sbuf(f"C2_fuw{i}", (128, CC, 128), BF16) for i in range(4)]
    fdw = [K.sbuf(f"C2_fdw{i}", (128, NHT, 128), BF16) for i in range(2)]
    tinf = K.sbuf("C2_tinf", (128, CC, TC), F32)
    ln3f = K.sbuf("C2_ln3f", (128, CC, TC), F32)
    tb16 = K.sbuf("C2_tb16", (128, CC, TC), BF16)
    tsq = K.sbuf("C2_tsq", (128, CC, TC), BF16)
    sg16c = K.sbuf("C2_sg16c", (128, 2, 512), BF16)
    h16c = K.sbuf("C2_h16c", (128, NHT, 512), BF16)
    srow2 = K.sbuf("C2_srow2", (1, 2 * TC), F32)
    vrow2 = K.sbuf("C2_vrow2", (1, 3 * TC), F32)
    PD = K.psum("C2_PD", (128, 2048), F32)
    PGU = K.psum("C2_PGU", (128, 1536), F32)
    K.dma("sync", gam1[:], d["gam1"][:], "c2CST")
    g1_pt = K.dma("sync", bet1[:], d["bet1"][:], "c2CST")

    ldq = [None, None]
    used = [None, None]
    wg_used = [None] * 4
    wd_used = [None, None]
    ev_stats = None
    prev_bc2 = None
    ev_gu = None
    hm_pt = None
    sg_prevs = [None, None]
    hm_prevs = [None, None]
    sgp = [None, None]
    ev_fo = None
    st_out = None
    prev_x2 = None
    psq2 = None
    for c in range(NCH):
        cb = c % 2
        tsl = slice(c * TC, (c + 1) * TC)
        if c == 0:
            K.dma("sync", x1c[0][:], x1_pct[:, :, tsl], "c2G0", wait=[bsx])
            K.dma("sync", aoc[0][:], aout_pct[:, :, tsl], "c2G0",
                  wait=[("cAOE0", K.cnt["cAOE0"]), ("cAOE1", K.cnt["cAOE1"])])
            K.dma("sync", xnc[0][:], xn1_pct[:, :, tsl], "c2G0",
                  wait=[("cSTx", K.cnt["cSTx"])])
            K.dma("sync", mrw[0][0:1, 0:TC], d["s_mr1"][0:1, tsl], "c2G0",
                  wait=[("cSTm", K.cnt["cSTm"])])
            ldq[0] = K.dma("sync", mrw[0][0:1, TC:2 * TC], d["s_mr1"][1:2, tsl], "c2G0")
        if c + 1 < NCH:
            nsl = slice((c + 1) * TC, (c + 2) * TC)
            nb_ = (c + 1) % 2
            w_ = [used[nb_], bsx, ("cAOE0", K.cnt["cAOE0"]),
                  ("cAOE1", K.cnt["cAOE1"]), ("cSTx", K.cnt["cSTx"]),
                  ("cSTm", K.cnt["cSTm"])]
            K.dma("sync", x1c[nb_][:], x1_pct[:, :, nsl], f"c2G{nb_}", wait=w_)
            K.dma("sync", aoc[nb_][:], aout_pct[:, :, nsl], f"c2G{nb_}")
            K.dma("sync", xnc[nb_][:], xn1_pct[:, :, nsl], f"c2G{nb_}")
            K.dma("sync", mrw[nb_][0:1, 0:TC], d["s_mr1"][0:1, nsl], f"c2G{nb_}")
            ldq[nb_] = K.dma("sync", mrw[nb_][0:1, TC:2 * TC], d["s_mr1"][1:2, nsl],
                             f"c2G{nb_}")
        b3 = K.mm(PD[:, 0:512], onesf[:], mrw[cb][0:1, 0:TC], True, True,
                  wait=[ldq[cb], ("VE", K.cnt["VE"])])
        b3 = K.mm(PD[:, 512:1024], onesf[:], mrw[cb][0:1, TC:2 * TC], True, True,
                  inc=True)
        m3v = PD[:, 0:512][:, None, :].broadcast_to((128, CC, 512))
        r3v = PD[:, 512:1024][:, None, :].broadcast_to((128, CC, 512))
        K.op("vector", "tensor_sub", wait=[b3, ldq[cb], st_out], out=tinf[:],
             in0=x1c[cb][:], in1=m3v)
        K.op("vector", "tensor_mul", out=tinf[:], in0=tinf[:], in1=r3v)
        tin_pt = K.op("vector", "tensor_add", inc=True, out=tinf[:],
                      in0=tinf[:], in1=aoc[cb][:])
        tb_pt = K.op("vector", "tensor_copy", out=tb16[:], in_=tinf[:])
        psq2 = K.act(tsq[:], tinf[:], AF.Square, wait=[tin_pt], inc=True)
        for i in range(CC):
            K.mm(PGU[0:1, 0:512], onesb[:], tb16[:, i, :], start=(i == 0),
                 stop=(i == CC - 1),
                 wait=[tb_pt, ev_stats, sg_prevs[0], hm_prevs[0]])
        mmst = None
        for i in range(CC):
            mmst = K.mm(PGU[0:1, 512:1024], onesb[:], tsq[:, i, :], start=(i == 0),
                        stop=(i == CC - 1), wait=[psq2, sg_prevs[1]],
                        inc=(i == CC - 1))
        K.act(srow2[0:1, 0:TC], PGU[0:1, 0:512], AF.Identity, scale=1.0 / D,
              wait=[mmst, prev_bc2])
        ev_stats = K.act(srow2[0:1, TC:2 * TC], PGU[0:1, 512:1024], AF.Identity,
                         scale=1.0 / D, inc=True)
        K.op("vector", "tensor_mul", wait=[ev_stats], out=vrow2[0:1, 0:TC],
             in0=srow2[0:1, 0:TC], in1=srow2[0:1, 0:TC])
        vv2 = K.op("vector", "tensor_sub", inc=True, out=vrow2[0:1, 0:TC],
                   in0=srow2[0:1, TC:2 * TC], in1=vrow2[0:1, 0:TC])
        sd2 = K.act(vrow2[0:1, TC:2 * TC], vrow2[0:1, 0:TC], AF.Sqrt,
                    bias=ceps[0:1, :], wait=[vv2], inc=True)
        rr2 = K.op("vector", "reciprocal", wait=[sd2], inc=True,
                   out=vrow2[0:1, 2 * TC:3 * TC], in_=vrow2[0:1, TC:2 * TC])
        b4 = K.mm(PD[:, 1024:1536], onesf[:], srow2[0:1, 0:TC], True, True,
                  wait=[rr2, ("VE", K.cnt["VE"])])
        b4 = K.mm(PD[:, 1536:2048], onesf[:], vrow2[0:1, 2 * TC:3 * TC], True, True,
                  inc=True)
        prev_bc2 = b4
        m4v = PD[:, 1024:1536][:, None, :].broadcast_to((128, CC, 512))
        r4v = PD[:, 1536:2048][:, None, :].broadcast_to((128, CC, 512))
        K.op("vector", "tensor_sub", wait=[b4], out=ln3f[:], in0=tinf[:], in1=m4v)
        K.op("vector", "tensor_mul", out=ln3f[:], in0=ln3f[:], in1=r4v)
        ln_pt = None
        for i in range(CC):
            ln_pt = K.op("vector", "tensor_scalar", inc=(i == CC - 1),
                         wait=([g1_pt] if i == 0 else ()),
                         out=ln3f[:, i, :], in0=ln3f[:, i, :],
                         scalar1=gam1[:, i:i + 1], scalar2=bet1[:, i:i + 1],
                         op0=ALU.mult, op1=ALU.add, const_scalars=True)
        for ht in range(NHT):
            hb = ht % 4
            pp = ht % 2
            lg = K.dma("gpsimd", fgw[hb][:], d["s_fg"][ht], f"c2W{hb}",
                       wait=[wg_used[hb], wp["PREP"]])
            lu = K.dma("gpsimd", fuw[hb][:], d["s_fu"][ht], f"c2W{hb}")
            greg = PGU[:, pp * 512:pp * 512 + 512]
            ureg = PGU[:, 1024:1536]
            gm = None
            for i in range(CC):
                gm = K.mm(greg, fgw[hb][:, i, :], xnc[cb][:, i, :],
                          start=(i == 0), stop=(i == CC - 1),
                          wait=(lu, ldq[cb], sg_prevs[pp],
                                ev_stats if ht < 2 else None),
                          inc=(i == CC - 1))
            sg_pt = K.act(sg16c[:, pp, :], greg, AF.Silu, scale=DEQ,
                          wait=[gm, hm_prevs[pp]], inc=True)
            sg_prevs[pp] = sg_pt
            sgp[pp] = sg_pt
            um = None
            for i in range(CC):
                um = K.mm(ureg, fuw[hb][:, i, :], xnc[cb][:, i, :],
                          start=(i == 0), stop=(i == CC - 1),
                          wait=(hm_pt,), inc=(i == CC - 1))
            hm_pt = K.op("vector", "tensor_mul", wait=[um, sgp[pp]], inc=True,
                         out=h16c[:, ht, :], in0=sg16c[:, pp, :], in1=ureg)
            hm_prevs[pp] = hm_pt
            ev_gu = hm_pt
            wg_used[hb] = um
        for wave in range(2):
            for dtw in range(4):
                dt = wave * 4 + dtw
                db = dt % 2
                ldd = K.dma("gpsimd", fdw[db][:], d["s_fd"][dt], f"c2D{db}",
                            wait=[wd_used[db], wp["PREP"]])
                reg = PD[:, dtw * 512:(dtw + 1) * 512]
                dn = None
                for hc in range(NHT):
                    dn = K.mm(reg, fdw[db][:, hc, :], h16c[:, hc, :],
                              start=(hc == 0), stop=(hc == NHT - 1),
                              wait=(ldd, hm_pt, ln_pt, ev_fo),
                              inc=(hc == NHT - 1))
                wd_used[db] = dn
                ev_fo = K.op("vector", "scalar_tensor_tensor",
                             wait=[dn, ln_pt, st_out], inc=True,
                             out=tinf[:, dt, :], in0=reg, scalar=DEQ * DEQ,
                             in1=ln3f[:, dt, :], op0=ALU.mult, op1=ALU.add)
        x2_pt = K.op("vector", "tensor_add", wait=[ev_fo], inc=True, out=tinf[:],
                     in0=tinf[:], in1=x1c[cb][:])
        prev_x2 = x2_pt
        used[cb] = x2_pt
        st_out = K.dma("sync", out_pct[:, :, tsl], tinf[:], "oOUT", wait=[x2_pt])
    return {"out": True}


PHASES = ["A", "L0", "B", "C"]


def body(K, d):
    wp = phase_prep(K, d)
    st = {}
    if "A" in PHASES:
        st.update(phase_A(K, d, wp) or {})
    if "L0" in PHASES:
        st.update(phase_L0(K, d, wp, st) or {})
    if "B" in PHASES:
        st.update(phase_B(K, d, wp, st) or {})
    if "C" in PHASES:
        st.update(phase_C(K, d, wp, st) or {})
    if "out" not in st:
        z = K.sbuf("zz_out", (128, 64), F32)
        zp = K.op("vector", "memset", z[:], 0.0, inc=True)
        K.dma("sync", d["out"][:].rearrange("c p t -> p c t")[:, 0:1, 0:64], z[:, None, :], "oOUT", wait=[zp])
    K.barrier()


RACECHECK = True


def build_program():
    nc = bass.Bass("TRN2", target_bir_lowering=False,
                   detect_race_conditions=RACECHECK)
    d = declare_tensors(nc)
    es = contextlib.ExitStack()
    K = KB(nc, es)
    for s in ALLSEMS:
        K.semh[s] = es.enter_context(nc.semaphore(s))
    K.cur = "__dry__"
    K.walk_reset()
    body(K, d)
    with nc.Block() as block:
        for engname, deco in (("sync", block.sync), ("gpsimd", block.gpsimd),
                              ("vector", block.vector), ("scalar", block.scalar),
                              ("tensor", block.tensor)):
            def runner(eng, engname=engname):
                K.cur = engname
                K.eng = eng
                K.walk_reset()
                body(K, d)
            deco(runner)
    es.close()
    return nc


def make_in_maps(inputs):
    h = host_prep(inputs)
    x = np.asarray(inputs["L0_attn_x"], dtype=np.float32)
    in_maps = []
    for c in range(NCORES):
        m = dict(h)
        xc = np.concatenate([x[2 * c].T, x[2 * c + 1].T], axis=1)
        m["xT"] = np.ascontiguousarray(xc.reshape(CC, 128, T))
        in_maps.append(m)
    return in_maps


def kernel(**inputs):
    nc = build_program()
    in_maps = make_in_maps(inputs)
    res = run_bass_kernel_spmd(nc, in_maps, core_ids=list(range(NCORES)))
    outs = []
    for c in range(NCORES):
        oT = np.asarray(res.results[c]["out"]).reshape(D, T)
        outs.append(oT[:, 0:S].T)
        outs.append(oT[:, S:].T)
    return np.stack(outs).astype(np.float32)


# version 19
# speedup vs baseline: 1.1594x; 1.1594x over previous
import contextlib
import math

import numpy as np
import ml_dtypes

import concourse.bass as bass
import concourse.mybir as mybir
from concourse.bass_utils import run_bass_kernel_spmd

F32 = mybir.dt.float32
BF16 = mybir.dt.bfloat16
I32 = mybir.dt.int32
AF = mybir.ActivationFunctionType
ALU = mybir.AluOpType
AX = mybir.AxisListType
BB = ml_dtypes.bfloat16

NCORES = 8
D = 1024
CC = 8
T = 4096
TC = 512
NCH = T // TC
NB = 2
S = 2048
GSZ = 256
NGRP = T // GSZ
HID = 2048
QKD = 128
NE = 16
M = 512
FFH = 3072
NHT = FFH // 128
NHEAD = 16
DH = 64
L1SCALE = 1.0 / 8.0
LNEPS = 1e-5
MOE_SCALE = 2.5
DEQ = 1.0 / 127.0

DEV = False
RELAX_LEVEL = 0

ENGS = ("sync", "gpsimd", "vector", "scalar", "tensor")
OWNSEM = {"tensor": "TE", "vector": "VE", "scalar": "SE", "gpsimd": "GE"}
ALLSEMS = ("TE", "VE", "SE", "GE")


class KB:

    def __init__(self, nc, es):
        self.nc = nc
        self._es = es
        self.cur = "__dry__"
        self.eng = None
        self.store = {}
        self.alloc_seq = []
        self.semh = {}
        self.dma_sems = []
        self.cnt = {}
        self.wd = {}
        self.prev_small = {}

    def walk_reset(self):
        self.cnt = {s: 0 for s in ALLSEMS}
        for s in self.dma_sems:
            self.cnt[s] = 0
        self.wd = {e: {} for e in ENGS}
        self.prev_small = {}

    def sbuf(self, name, shape, dtype):
        if name not in self.store:
            assert self.cur == "__dry__", f"late alloc {name}"
            g = self.nc.sbuf_tensor(name, list(shape), dtype)
            h = g.__enter__()
            self.store[name] = (g, h)
            self.alloc_seq.append(name)
        return self.store[name][1]

    def psum(self, name, shape, dtype=F32):
        if name not in self.store:
            assert self.cur == "__dry__", f"late alloc {name}"
            g = self.nc.psum_tensor(name, list(shape), dtype)
            h = g.__enter__()
            self.store[name] = (g, h)
            self.alloc_seq.append(name)
        return self.store[name][1]

    def free(self, *names):
        if self.cur == "__dry__":
            order = {n: i for i, n in enumerate(self.alloc_seq)}
            for n in sorted(names, key=lambda n: -order[n]):
                g, _ = self.store[n]
                g.__exit__(None, None, None)

    def wait(self, engine, pt):
        if pt is None:
            return
        sem, val = pt
        if OWNSEM.get(engine) == sem:
            return
        if sem not in self.semh:
            assert self.cur == "__dry__", f"late sem {sem}"
            self.semh[sem] = self._es.enter_context(self.nc.semaphore(sem))
            self.dma_sems.append(sem)
        if val <= self.wd[engine].get(sem, 0):
            return
        self.wd[engine][sem] = val
        if self.cur == engine:
            self.eng.wait_ge(self.semh[sem], val)

    def barrier(self, engines=ENGS):
        pts = [(s, v) for s, v in self.cnt.items() if v > 0 and s != "PREP"]
        for e in engines:
            for pt in pts:
                self.wait(e, pt)

    def dma(self, q, out, in_, sem, wait=()):
        for pt in wait:
            self.wait(q, pt)
        if sem not in self.semh:
            assert self.cur == "__dry__", f"late sem {sem}"
            self.semh[sem] = self._es.enter_context(self.nc.semaphore(sem))
            self.dma_sems.append(sem)
        if sem not in self.cnt:
            self.cnt[sem] = 0
        n = 32 if ("DRam" in type(out.tensor).__name__
                   and "DRam" in type(in_.tensor).__name__) else 16
        self.cnt[sem] += n
        if self.cur == q:
            self.eng.dma_start(out=out, in_=in_).then_inc(self.semh[sem], n)
        return (sem, self.cnt[sem])

    def op(self, engine, opname, *a, wait=(), swait=(), inc=False,
           const_scalars=False, **kw):
        for pt in wait:
            self.wait(engine, pt)
        for pt in swait:
            if pt is None:
                continue
            sem_, val_ = pt
            if val_ > self.wd[engine].get(sem_, 0):
                self.wd[engine][sem_] = val_
                if self.cur == engine:
                    self.eng.wait_ge(self.semh[sem_], val_)
        sem = OWNSEM[engine]
        if engine in ("vector", "scalar", "gpsimd"):
            def _ap(x):
                return hasattr(x, "tensor")
            apscal = (not const_scalars) and any(
                _ap(kw.get(k)) for k in ("scalar1", "scalar2", "bias", "scale"))
            need = apscal or self.prev_small.get(engine, False)
            out_ap = kw.get("out")
            if out_ap is None and a:
                out_ap = a[0]
            try:
                import math as _m
                elems = 1
                for s_ in out_ap.shape[1:]:
                    elems *= s_
            except Exception:
                elems = 0
            self.prev_small[engine] = elems < 256 or (
                getattr(out_ap, "shape", (128,))[0] < 32)
            prev = self.cnt[sem]
            self.cnt[sem] += 1
            pt_out = (sem, self.cnt[sem])
            if self.cur == engine:
                if need and prev > 0:
                    self.eng.wait_ge(self.semh[sem], prev)
                ins = getattr(self.eng, opname)(*a, **kw)
                ins.then_inc(self.semh[sem], 1)
            return pt_out
        pt_out = None
        if inc:
            self.cnt[sem] += 1
            pt_out = (sem, self.cnt[sem])
        if self.cur == engine:
            ins = getattr(self.eng, opname)(*a, **kw)
            if inc:
                ins.then_inc(self.semh[sem], 1)
        return pt_out

    def mm(self, out, lhsT, rhs, start, stop, wait=(), inc=False):
        return self.op("tensor", "matmul", wait=wait, inc=inc,
                       out=out, lhsT=lhsT, rhs=rhs, start=start, stop=stop)

    def tr(self, out, in_, ident, wait=(), inc=False):
        return self.op("tensor", "transpose", wait=wait, inc=inc,
                       out=out, in_=in_, identity=ident)

    def act(self, out, in_, func, bias=0.0, scale=1.0, wait=(), inc=False,
            const_scalars=True):
        return self.op("scalar", "activation", wait=wait, inc=inc,
                       const_scalars=const_scalars,
                       out=out, in_=in_, func=func, bias=bias, scale=scale)


def _rel_bias_matrix(rel_emb):
    NUM_BUCKETS, MAX_DIST = 32, 128
    pos = np.arange(GSZ)
    n = -(pos[None, :] - pos[:, None])
    nb = NUM_BUCKETS // 2
    ret = (n < 0).astype(np.int64) * nb
    n = np.abs(n)
    max_exact = nb // 2
    is_small = n < max_exact
    val_large = max_exact + (
        np.log(np.maximum(n, 1).astype(np.float32) / max_exact)
        / math.log(MAX_DIST / max_exact) * (nb - max_exact)
    ).astype(np.int64)
    val_large = np.minimum(val_large, nb - 1)
    bucket = ret + np.where(is_small, n, val_large)
    return rel_emb[bucket, 0]


def host_prep(inp):
    h = {}

    def lhsT_layout(w, name):
        C, F = w.shape
        h[name] = np.ascontiguousarray(
            w.reshape(C // 128, 128, F // 128, 128).transpose(1, 0, 2, 3))

    def rhs_layout(w, name):
        C, F = w.shape
        h[name] = np.ascontiguousarray(w.reshape(C // 128, 128, F))

    lhsT_layout(inp["L0_attn_W_qk"], "wqk")
    rhs_layout(inp["L0_attn_W_h"][:, :HID], "whv")
    lhsT_layout(inp["L0_attn_W_h"][:, HID:], "whg")
    lhsT_layout(inp["L0_attn_W_o"], "wo")
    h["bhv"] = np.ascontiguousarray(inp["L0_attn_b_h"][:HID].reshape(1, HID))
    h["bhg"] = np.ascontiguousarray(inp["L0_attn_b_h"][HID:].reshape(16, 128).T)
    h["bqk"] = np.ascontiguousarray(inp["L0_attn_b_qk"].reshape(1, 128).T)
    h["qkg"] = np.ascontiguousarray(inp["L0_attn_qk_gamma"].T)
    h["qkb"] = np.ascontiguousarray(inp["L0_attn_qk_beta"].T)
    h["lng"] = np.ascontiguousarray(inp["L0_attn_norm_g"].reshape(8, 128).T)
    h["lnb"] = np.ascontiguousarray(inp["L0_attn_norm_b"].reshape(8, 128).T)
    h["bo"] = np.ascontiguousarray(inp["L0_attn_b_o"].reshape(8, 128).T)
    bT = _rel_bias_matrix(np.asarray(inp["L0_attn_rel_emb"])).T
    h["biasT"] = np.ascontiguousarray(bT.reshape(2, 128, 256).transpose(1, 0, 2))

    h["gwT"] = np.ascontiguousarray(
        inp["L0_ffn_gate_w"].T.reshape(8, 128, 16).transpose(1, 0, 2))

    def expw_gu(w):
        return w.reshape(8, 128, 4, 128).transpose(1, 2, 0, 3)

    def expw_d(w):
        return w.reshape(4, 128, 8, 128).transpose(1, 2, 0, 3)

    h["mwg"] = np.ascontiguousarray(np.stack([expw_gu(inp["L0_ffn_w_gate"][e]) for e in range(NE)]))
    h["mwu"] = np.ascontiguousarray(np.stack([expw_gu(inp["L0_ffn_w_up"][e]) for e in range(NE)]))
    h["mwd"] = np.ascontiguousarray(np.stack([expw_d(inp["L0_ffn_w_down"][e]) for e in range(NE)]))
    h["wsg"] = np.ascontiguousarray(expw_gu(inp["L0_ffn_ws_gate"]))
    h["wsu"] = np.ascontiguousarray(expw_gu(inp["L0_ffn_ws_up"]))
    h["wsd"] = np.ascontiguousarray(expw_d(inp["L0_ffn_ws_down"]))

    lhsT_layout(inp["L1_attn_Wq"], "wq1")
    lhsT_layout(inp["L1_attn_Wk"], "wk1")
    rhs_layout(inp["L1_attn_Wv"], "wv1")
    lhsT_layout(inp["L1_attn_Wo"], "wo1")
    h["gam1"] = np.ascontiguousarray(inp["L1_attn_gamma"].reshape(8, 128).T)
    h["bet1"] = np.ascontiguousarray(inp["L1_attn_beta"].reshape(8, 128).T)

    wgT = inp["L1_ffn_w_gate"].T
    wuT = inp["L1_ffn_w_up"].T
    wdT = inp["L1_ffn_w_down"].T
    h["fg"] = np.ascontiguousarray(wgT.reshape(8, 128, 24, 128).transpose(2, 1, 0, 3))
    h["fu"] = np.ascontiguousarray(wuT.reshape(8, 128, 24, 128).transpose(2, 1, 0, 3))
    h["fd"] = np.ascontiguousarray(wdT.reshape(24, 128, 8, 128).transpose(2, 1, 0, 3))

    h["idf"] = np.eye(128, dtype=np.float32)
    h["idb"] = np.eye(128).astype(BB)
    h["onesb"] = np.ones((128, 1), dtype=BB)
    h["onesf"] = np.ones((1, 128), dtype=np.float32)
    h["onesr2k"] = np.ones((1, S), dtype=BB)
    return h



def declare_tensors(nc):
    d = {}

    def din(name, shape, dtype):
        d[name] = nc.dram_tensor(name, list(shape), dtype, kind="ExternalInput").ap()

    def dscr(name, shape, dtype):
        d[name] = nc.dram_tensor(name, list(shape), dtype, kind="Internal").ap()

    din("xT", (CC, 128, T), F32)
    din("wqk", (128, 8, 1, 128), F32)
    din("whv", (8, 128, HID), F32)
    din("whg", (128, 8, 16, 128), F32)
    din("wo", (128, 16, 8, 128), F32)
    din("bhv", (1, HID), F32)
    din("bhg", (128, 16), F32)
    din("bqk", (128, 1), F32)
    din("qkg", (128, 4), F32)
    din("qkb", (128, 4), F32)
    din("lng", (128, 8), F32)
    din("lnb", (128, 8), F32)
    din("bo", (128, 8), F32)
    din("biasT", (128, 2, 256), F32)
    din("gwT", (128, 8, 16), F32)
    din("mwg", (NE, 128, 4, 8, 128), F32)
    din("mwu", (NE, 128, 4, 8, 128), F32)
    din("mwd", (NE, 128, 8, 4, 128), F32)
    din("wsg", (128, 4, 8, 128), F32)
    din("wsu", (128, 4, 8, 128), F32)
    din("wsd", (128, 8, 4, 128), F32)
    din("wq1", (128, 8, 8, 128), F32)
    din("wk1", (128, 8, 8, 128), F32)
    din("wv1", (8, 128, 1024), F32)
    din("wo1", (128, 8, 8, 128), F32)
    din("gam1", (128, 8), F32)
    din("bet1", (128, 8), F32)
    din("fg", (NHT, 128, 8, 128), I32)
    din("fu", (NHT, 128, 8, 128), I32)
    din("fd", (8, 128, NHT, 128), I32)
    din("idf", (128, 128), F32)
    din("idb", (128, 128), BF16)
    din("onesb", (128, 1), BF16)
    din("onesf", (1, 128), F32)
    din("onesr2k", (1, S), BF16)

    d["out"] = nc.dram_tensor("out", [CC, 128, T], F32, kind="ExternalOutput").ap()

    dscr("s_wqk", (128, 8, 1, 128), BF16)
    dscr("s_whv", (8, 128, HID), BF16)
    dscr("s_whg", (128, 8, 16, 128), BF16)
    dscr("s_wo", (128, 16, 8, 128), BF16)
    dscr("s_bhv", (1, HID), BF16)
    dscr("s_mwg", (NE, 128, 4, 8, 128), BF16)
    dscr("s_mwu", (NE, 128, 4, 8, 128), BF16)
    dscr("s_mwd", (NE, 128, 8, 4, 128), BF16)
    dscr("s_wsg", (128, 4, 8, 128), BF16)
    dscr("s_wsu", (128, 4, 8, 128), BF16)
    dscr("s_wsd", (128, 8, 4, 128), BF16)
    dscr("s_wq1", (128, 8, 8, 128), BF16)
    dscr("s_wk1", (128, 8, 8, 128), BF16)
    dscr("s_wv1", (8, 128, 1024), BF16)
    dscr("s_wo1", (128, 8, 8, 128), BF16)
    dscr("s_fg", (NHT, 128, 8, 128), BF16)
    dscr("s_fu", (NHT, 128, 8, 128), BF16)
    dscr("s_fd", (8, 128, NHT, 128), BF16)

    dscr("s_route", (16, T), F32)
    dscr("s_vt", (NB, 16, 128, HID), BF16)
    dscr("s_crow", (2, 16, 128), BF16)
    dscr("s_mr", (3, T), F32)
    dscr("s_mr1", (2, T), F32)
    dscr("s_xnb", (CC, 128, T), BF16)
    dscr("s_nmb", (CC, 128, T), BF16)
    dscr("s_moe", (CC, 128, T), F32)
    dscr("s_x1", (CC, 128, T), F32)
    dscr("s_aout", (CC, 128, T), F32)
    dscr("s_xn1b", (CC, 128, T), BF16)

    if DEV:
        d["dbg_xn"] = nc.dram_tensor("dbg_xn", [CC, 128, T], F32, kind="ExternalOutput").ap()
        d["dbg_route"] = nc.dram_tensor("dbg_route", [16, T], F32, kind="ExternalOutput").ap()
        d["dbg_nmb"] = nc.dram_tensor("dbg_nmb", [CC, 128, T], F32, kind="ExternalOutput").ap()
        d["dbg_qk"] = nc.dram_tensor("dbg_qk", [128, T], F32, kind="ExternalOutput").ap()
        d["dbg_scs"] = nc.dram_tensor("dbg_scs", [16, T], F32, kind="ExternalOutput").ap()
        d["dbg_stok"] = nc.dram_tensor("dbg_stok", [128, 16], F32, kind="ExternalOutput").ap()
        d["dbg_rtok"] = nc.dram_tensor("dbg_rtok", [128, 16], F32, kind="ExternalOutput").ap()
        d["dbg_x1"] = nc.dram_tensor("dbg_x1", [CC, 128, T], F32, kind="ExternalOutput").ap()
        d["dbg_x1f"] = nc.dram_tensor("dbg_x1f", [CC, 128, T], F32, kind="ExternalOutput").ap()
    return d


def phase_prep(K, d):
    pts = {}
    pts["wqk"] = K.dma("gpsimd", d["s_wqk"][:], d["wqk"][:], "pWQK")
    pts["bhv"] = K.dma("gpsimd", d["s_bhv"][:], d["bhv"][:], "pBHV")
    pts["whv"] = K.dma("gpsimd", d["s_whv"][:], d["whv"][:], "pWHV")
    pts["whg"] = K.dma("gpsimd", d["s_whg"][:], d["whg"][:], "pWHG")
    pts["wo"] = K.dma("gpsimd", d["s_wo"][:], d["wo"][:], "pWO")
    for e in range(NE):
        gate = [("aSTn", 16 * min(NCH, max(1, e)))]
        K.dma("gpsimd", d["s_mwg"][e], d["mwg"][e], "PREP", wait=gate)
        K.dma("gpsimd", d["s_mwu"][e], d["mwu"][e], "PREP")
        p3 = K.dma("gpsimd", d["s_mwd"][e], d["mwd"][e], "PREP")
        pts[f"moe{e}"] = p3
    K.dma("gpsimd", d["s_wsg"][:], d["wsg"][:], "PREP")
    K.dma("gpsimd", d["s_wsu"][:], d["wsu"][:], "PREP")
    pts["moeS"] = K.dma("gpsimd", d["s_wsd"][:], d["wsd"][:], "PREP")
    gate1 = [("bAC0", 32)]
    K.dma("gpsimd", d["s_wq1"][:], d["wq1"][:], "PREP", wait=gate1)
    K.dma("gpsimd", d["s_wk1"][:], d["wk1"][:], "PREP")
    K.dma("gpsimd", d["s_wv1"][:], d["wv1"][:], "PREP")
    K.dma("gpsimd", d["s_wo1"][:], d["wo1"][:], "PREP")
    K.dma("gpsimd", d["s_fg"][:], d["fg"][:], "PREP")
    K.dma("gpsimd", d["s_fu"][:], d["fu"][:], "PREP")
    pts["all"] = K.dma("gpsimd", d["s_fd"][:], d["fd"][:], "PREP")
    pts["PREP"] = ("PREP", K.cnt["PREP"])
    return pts

def phase_A(K, d, wp):
    onesb = K.sbuf("c_onesb", (128, 1), BF16)
    onesf = K.sbuf("c_onesf", (1, 128), F32)
    idf = K.sbuf("c_idf", (128, 128), F32)
    gwT = K.sbuf("c_gwT", (128, 8, 16), F32)
    wqk = K.sbuf("c_wqk16", (128, 8, 1, 128), BF16)
    bqk = K.sbuf("c_bqk", (128, 1), F32)
    qkg = K.sbuf("c_qkg", (128, 4), F32)
    qkb = K.sbuf("c_qkb", (128, 4), F32)
    lng = K.sbuf("c_lng", (128, 8), F32)
    lnb = K.sbuf("c_lnb", (128, 8), F32)
    for sb, dr in ((onesb, "onesb"), (onesf, "onesf"), (idf, "idf"), (gwT, "gwT"),
                   (bqk, "bqk"), (qkg, "qkg"), (qkb, "qkb"), (lng, "lng"), (lnb, "lnb")):
        cst_pt = K.dma("sync", sb[:], d[dr][:], "aCST")
    ceps = K.sbuf("c_eps", (128, 1), F32)
    eps_pt = K.op("vector", "memset", ceps[:], LNEPS, inc=True)
    qqT = K.sbuf("qqT", (128, T), BF16)
    qkT2 = K.sbuf("qkT2", (128, T), BF16)
    lqT = K.sbuf("lqT", (128, T), BF16)
    ltok = K.sbuf("ltok", (128, T), BF16)
    wqk_pt = K.dma("sync", wqk[:], d["s_wqk"][:], "aWQK", wait=[wp["wqk"]])

    routeT = K.sbuf("routeT", (16, T), F32)
    scraw = K.sbuf("A_scraw", (16, T), F32)
    xf = [K.sbuf(f"A_xf{i}", (128, CC, TC), F32) for i in range(2)]
    xb16 = [K.sbuf(f"A_xb16{i}", (128, CC, TC), BF16) for i in range(2)]
    xsq = [K.sbuf(f"A_xsq{i}", (128, CC, TC), BF16) for i in range(2)]
    srow = [K.sbuf(f"A_srow{i}", (1, 2 * TC), F32) for i in range(2)]
    vrow = [K.sbuf(f"A_vrow{i}", (1, 5 * TC), F32) for i in range(2)]
    pst = K.psum("A_pst", (1, 4 * TC), F32)
    xT_pct = d["xT"][:].rearrange("c p t -> p c t")
    ld = [None] * NCH
    done = [None, None]
    st_mr = [None, None]
    ev_stats = [None, None]
    ld[0] = K.dma("sync", xf[0][:], xT_pct[:, :, 0:TC], "aXF0", wait=[cst_pt])
    for c in range(NCH):
        cb = c % 2
        tsl = slice(c * TC, (c + 1) * TC)
        if c + 1 < NCH:
            ld[c + 1] = K.dma("sync", xf[(c + 1) % 2][:],
                              xT_pct[:, :, (c + 1) * TC:(c + 2) * TC],
                              f"aXF{(c + 1) % 2}", wait=[done[(c + 1) % 2]])
        pcp = K.op("vector", "tensor_copy", wait=[ld[c]], inc=True,
                   out=xb16[cb][:], in_=xf[cb][:])
        psq = K.act(xsq[cb][:], xf[cb][:], AF.Square, wait=[ld[c]], inc=True)
        done[cb] = psq
        po = cb * 1024
        for i in range(CC):
            K.mm(pst[0:1, po:po + 512], onesb[:], xb16[cb][:, i, :], start=(i == 0),
                 stop=(i == CC - 1), wait=[pcp, ev_stats[cb]])
        mmst = None
        for i in range(CC):
            mmst = K.mm(pst[0:1, po + 512:po + 1024], onesb[:], xsq[cb][:, i, :],
                        start=(i == 0), stop=(i == CC - 1), wait=[psq],
                        inc=(i == CC - 1))
        K.act(srow[cb][0:1, 0:TC], pst[0:1, po:po + 512], AF.Identity,
              scale=1.0 / D, wait=[mmst, st_mr[cb]])
        ev_stats[cb] = K.act(srow[cb][0:1, TC:2 * TC], pst[0:1, po + 512:po + 1024],
                             AF.Identity, scale=1.0 / D, inc=True)
        K.dma("sync", d["s_mr"][0:1, tsl], srow[cb][0:1, 0:TC], f"aSTm{cb}",
              wait=[ev_stats[cb]])
        K.op("vector", "tensor_mul", wait=[ev_stats[cb]], out=vrow[cb][0:1, 0:TC],
             in0=srow[cb][0:1, 0:TC], in1=srow[cb][0:1, 0:TC])
        vv = K.op("vector", "tensor_sub", inc=True, out=vrow[cb][0:1, 0:TC],
                  in0=srow[cb][0:1, TC:2 * TC], in1=vrow[cb][0:1, 0:TC])
        sd = K.act(vrow[cb][0:1, TC:2 * TC], vrow[cb][0:1, 0:TC], AF.Sqrt,
                   bias=ceps[0:1, :], wait=[vv], inc=True)
        rr = K.op("vector", "reciprocal", wait=[sd], inc=True,
                  out=vrow[cb][0:1, 2 * TC:3 * TC], in_=vrow[cb][0:1, TC:2 * TC])
        K.dma("sync", d["s_mr"][1:2, tsl], vrow[cb][0:1, 2 * TC:3 * TC],
              f"aSTm{cb}", wait=[rr])
        K.op("vector", "tensor_mul", wait=[rr], out=vrow[cb][0:1, 3 * TC:4 * TC],
             in0=vrow[cb][0:1, 2 * TC:3 * TC], in1=vrow[cb][0:1, 2 * TC:3 * TC])
        v2 = K.op("vector", "tensor_mul", inc=True, out=vrow[cb][0:1, 3 * TC:4 * TC],
                  in0=vrow[cb][0:1, 3 * TC:4 * TC], in1=vrow[cb][0:1, 0:TC])
        sd2 = K.act(vrow[cb][0:1, TC:2 * TC], vrow[cb][0:1, 3 * TC:4 * TC], AF.Sqrt,
                    bias=ceps[0:1, :], wait=[v2], inc=True)
        r2 = K.op("vector", "reciprocal", wait=[sd2], inc=True,
                  out=vrow[cb][0:1, 4 * TC:5 * TC], in_=vrow[cb][0:1, TC:2 * TC])
        st_mr[cb] = K.dma("sync", d["s_mr"][2:3, tsl], vrow[cb][0:1, 4 * TC:5 * TC],
                          f"aSTm{cb}", wait=[r2])
    K.barrier(("sync", "vector", "scalar", "tensor"))
    K.free("A_xb160", "A_xb161", "A_xsq0", "A_xsq1", "A_srow0", "A_srow1",
           "A_vrow0", "A_vrow1", "A_pst")

    xnf = K.sbuf("A_xnf", (128, CC, TC), F32)
    nmp = K.sbuf("A_nmp", (128, CC, TC), F32)
    nmb = K.sbuf("A_nmb", (128, CC, TC), BF16)
    xnb = K.sbuf("A_xnb", (128, CC, TC), BF16)
    qk16 = K.sbuf("A_qk16", (128, TC), BF16)
    lkT = K.sbuf("A_lkT", (128, TC), F32)
    mr3 = [K.sbuf(f"A_mr3{i}", (1, 3 * TC), F32) for i in range(2)]
    psb = K.psum("A_psb", (128, 3 * 512), F32)
    pssc = K.psum("A_pssc", (16, 512), F32)
    psqk = K.psum("A_psqk", (128, 512), F32)
    pstr = K.psum("A_pstr", (128, 128), F32)
    ld = [None] * NCH
    ldm = [None, None]
    xf_done = [None, None]
    rd_bcast = None
    ev_router = None
    ev_qk = None
    rd_tr = None
    st_xnb = None
    st_nmb = None
    prev_rt = None
    prev_qmm = None
    mrf = [(f"aSTm{i}", K.cnt[f"aSTm{i}"]) for i in range(2)]
    ld[0] = K.dma("sync", xf[0][:], xT_pct[:, :, 0:TC], "aXF0")
    K.dma("sync", mr3[0][0:1, 0:TC], d["s_mr"][0:1, 0:TC], "aMR0", wait=mrf)
    K.dma("sync", mr3[0][0:1, TC:2 * TC], d["s_mr"][1:2, 0:TC], "aMR0")
    ldm[0] = K.dma("sync", mr3[0][0:1, 2 * TC:3 * TC], d["s_mr"][2:3, 0:TC], "aMR0")
    for c in range(NCH):
        cb = c % 2
        tsl = slice(c * TC, (c + 1) * TC)
        if c + 1 < NCH:
            nsl = slice((c + 1) * TC, (c + 2) * TC)
            nb_ = (c + 1) % 2
            ld[c + 1] = K.dma("sync", xf[nb_][:], xT_pct[:, :, nsl],
                              f"aXF{nb_}", wait=[xf_done[nb_]])
            K.dma("sync", mr3[nb_][0:1, 0:TC], d["s_mr"][0:1, nsl],
                  f"aMR{nb_}", wait=mrf + [xf_done[nb_]])
            K.dma("sync", mr3[nb_][0:1, TC:2 * TC], d["s_mr"][1:2, nsl], f"aMR{nb_}")
            ldm[nb_] = K.dma("sync", mr3[nb_][0:1, 2 * TC:3 * TC], d["s_mr"][2:3, nsl],
                             f"aMR{nb_}")
        K.mm(psb[:, 0:512], onesf[:], mr3[cb][0:1, 0:TC], True, True,
             wait=[ldm[cb], rd_bcast])
        K.mm(psb[:, 512:1024], onesf[:], mr3[cb][0:1, TC:2 * TC], True, True)
        bc = K.mm(psb[:, 1024:1536], onesf[:], mr3[cb][0:1, 2 * TC:3 * TC],
                  True, True, inc=True)
        mb = psb[:, 0:512][:, None, :].broadcast_to((128, CC, 512))
        rb = psb[:, 512:1024][:, None, :].broadcast_to((128, CC, 512))
        r2b = psb[:, 1024:1536][:, None, :].broadcast_to((128, CC, 512))
        sub_pt = K.op("vector", "tensor_sub", wait=[bc, ld[c], prev_rt], inc=True,
                      out=xnf[:], in0=xf[cb][:], in1=mb)
        xf_done[cb] = sub_pt
        xnpt = K.op("vector", "tensor_mul", inc=True, out=xnf[:], in0=xnf[:], in1=rb)
        xnbpt = K.op("vector", "tensor_copy", wait=[st_xnb], inc=True,
                     out=xnb[:], in_=xnf[:])
        st_xnb = K.dma("sync", d["s_xnb"][:].rearrange("c p t -> p c t")[:, :, tsl],
                       xnb[:], "aSTx", wait=[xnbpt])
        rt = None
        for i in range(CC):
            rt = K.mm(pssc[:], gwT[:, i, :], xnf[:, i, :], start=(i == 0),
                      stop=(i == CC - 1), wait=[xnpt, ev_router], inc=(i == CC - 1))
        prev_rt = rt
        ev_router = K.act(scraw[:, tsl], pssc[:], AF.Identity, wait=[rt], inc=True)
        rd_bcast = K.op("vector", "tensor_mul", wait=[xnpt], inc=True,
                        out=nmp[:], in0=xnf[:], in1=r2b)
        npt = None
        for i in range(CC):
            npt = K.op("vector", "tensor_scalar", inc=(i == CC - 1),
                       wait=([st_nmb, prev_qmm] if i == 0 else ()),
                       out=nmb[:, i, :], in0=nmp[:, i, :],
                       scalar1=lng[:, i:i + 1], scalar2=lnb[:, i:i + 1],
                       op0=ALU.mult, op1=ALU.add, const_scalars=True)
        st_nmb = K.dma("sync", d["s_nmb"][:].rearrange("c p t -> p c t")[:, :, tsl],
                       nmb[:], "aSTn", wait=[npt])
        qmm = None
        for i in range(CC):
            qmm = K.mm(psqk[:], wqk[:, i, 0, :], nmb[:, i, :], start=(i == 0),
                       stop=(i == CC - 1), wait=[npt, wqk_pt, ev_qk],
                       inc=(i == CC - 1))
        prev_qmm = qmm
        qk_pt = K.act(qk16[:], psqk[:], AF.Silu, bias=bqk[:], wait=[qmm], inc=True)
        ev_qk = qk_pt
        K.op("vector", "tensor_scalar", wait=[qk_pt], out=qqT[:, tsl], in0=qk16[:],
             scalar1=qkg[:, 0:1], scalar2=qkb[:, 0:1], op0=ALU.mult, op1=ALU.add, const_scalars=True)
        K.op("vector", "tensor_scalar", out=lqT[:, tsl], in0=qk16[:],
             scalar1=qkg[:, 1:2], scalar2=qkb[:, 1:2], op0=ALU.mult, op1=ALU.add, const_scalars=True)
        K.op("vector", "tensor_scalar", out=qkT2[:, tsl], in0=qk16[:],
             scalar1=qkg[:, 2:3], scalar2=qkb[:, 2:3], op0=ALU.mult, op1=ALU.add, const_scalars=True)
        lk_pt = K.op("vector", "tensor_scalar", inc=True, out=lkT[:], in0=qk16[:],
                     scalar1=qkg[:, 3:4], scalar2=qkb[:, 3:4], op0=ALU.mult, op1=ALU.add, const_scalars=True)
        for j in range(4):
            tp = K.tr(pstr[:], lkT[:, j * 128:(j + 1) * 128], idf[:],
                      wait=[lk_pt, rd_tr], inc=True)
            rd_tr = K.op("vector", "tensor_copy", wait=[tp], inc=True,
                         out=ltok[:, (c * 4 + j) * 128:(c * 4 + j + 1) * 128],
                         in_=pstr[:])
        if DEV:
            p1 = K.dma("sync", d["dbg_xn"][:].rearrange("c p t -> p c t")[:, :, tsl],
                       xnf[:], "zDBG", wait=[xnpt])
            p2 = K.dma("gpsimd", d["dbg_nmb"][:].rearrange("c p t -> p c t")[:, :, tsl],
                       nmb[:], "zDBG2", wait=[npt])
            p3 = K.dma("gpsimd", d["dbg_qk"][:, tsl], qk16[:], "zDBG2", wait=[qk_pt])
            K.barrier()

    K.barrier()
    scs = K.sbuf("A_scs", (16, T), F32)
    sg_pt = K.act(scs[:], scraw[:], AF.Sigmoid, inc=True)
    stok = K.sbuf("A_stok", (128, 16), F32)
    m1t = K.sbuf("A_m1t", (128, 1), F32)
    m2t = K.sbuf("A_m2t", (128, 1), F32)
    msk = K.sbuf("A_msk", (128, 16), F32)
    mski = K.sbuf("A_mski", (128, 16), I32)
    m1b = K.sbuf("A_m1b", (128, 16), F32)
    rtok = K.sbuf("A_rtok", (128, 16), F32)
    wqt = K.sbuf("A_wqt", (128, 1), F32)
    ones16 = K.sbuf("A_ones16", (128, 16), F32)
    o16 = K.op("vector", "memset", ones16[:], 1.0, inc=True)
    rd1 = None
    rd2 = None
    for tt in range(T // 128):
        csl = slice(tt * 128, (tt + 1) * 128)
        tp1 = K.tr(pstr[:, 0:16], scs[:, csl], idf[0:16, 0:16], wait=[sg_pt, rd1], inc=True)
        p = K.op("vector", "tensor_copy", wait=[tp1, o16], inc=True,
                 out=stok[:], in_=pstr[:, 0:16])
        rd1 = p
        p = K.op("vector", "reduce_max", swait=[p], inc=True,
                 out=m1t[:], in_=stok[:], axis=AX.X)
        p = K.op("vector", "tensor_scalar", swait=[p], inc=True, out=m1b[:],
                 in0=ones16[:], scalar1=m1t[:], scalar2=0.0, op0=ALU.mult, op1=ALU.add)
        p = K.op("vector", "tensor_tensor", swait=[p], inc=True, out=mski[:],
                 in0=stok[:], in1=m1b[:], op=ALU.is_ge)
        p = K.op("vector", "tensor_copy", swait=[p], inc=True, out=msk[:], in_=mski[:])
        p = K.op("vector", "tensor_scalar", swait=[p], inc=True, out=msk[:],
                 in0=msk[:], scalar1=1e30, scalar2=None, op0=ALU.mult)
        p = K.op("vector", "tensor_sub", swait=[p], inc=True, out=msk[:],
                 in0=stok[:], in1=msk[:])
        p = K.op("vector", "reduce_max", swait=[p], inc=True,
                 out=m2t[:], in_=msk[:], axis=AX.X)
        p = K.op("vector", "tensor_add", swait=[p], inc=True, out=wqt[:],
                 in0=m1t[:], in1=m2t[:])
        p = K.op("vector", "tensor_scalar", swait=[p], inc=True, out=wqt[:],
                 in0=wqt[:], scalar1=1e-20, scalar2=None, op0=ALU.add)
        p = K.op("vector", "reciprocal", swait=[p], inc=True, out=wqt[:], in_=wqt[:])
        p = K.op("vector", "tensor_scalar", swait=[p], inc=True, out=wqt[:],
                 in0=wqt[:], scalar1=MOE_SCALE, scalar2=None, op0=ALU.mult)
        p = K.op("vector", "tensor_scalar", swait=[p], inc=True, out=rtok[:],
                 in0=ones16[:], scalar1=m2t[:], scalar2=0.0, op0=ALU.mult, op1=ALU.add)
        p = K.op("vector", "tensor_tensor", swait=[p], inc=True, out=mski[:],
                 in0=stok[:], in1=rtok[:], op=ALU.is_ge)
        p = K.op("vector", "tensor_copy", swait=[p], inc=True, out=rtok[:], in_=mski[:])
        p = K.op("vector", "tensor_mul", swait=[p], inc=True, out=rtok[:],
                 in0=rtok[:], in1=stok[:])
        rt2 = K.op("vector", "tensor_scalar", swait=[p], inc=True, out=rtok[:],
                   in0=rtok[:], scalar1=wqt[:], scalar2=None, op0=ALU.mult)
        tp2 = K.tr(psqk[0:16, 0:128], rtok[:], idf[:], wait=[rt2, rd2], inc=True)
        rd2 = K.op("vector", "tensor_copy", wait=[tp2], swait=[rt2], inc=True,
                   out=routeT[:, csl], in_=psqk[0:16, 0:128])
        if DEV and tt == 0:
            K.dma("sync", d["dbg_stok"][:], stok[:], "zDBG", wait=[rd1])
            K.dma("sync", d["dbg_rtok"][:], rtok[:], "zDBG", wait=[rt2])
            K.barrier()
    K.dma("sync", d["s_route"][:], routeT[:], "aSTr", wait=[rd2])
    if DEV:
        K.dma("sync", d["dbg_route"][:], routeT[:], "zDBG", wait=[rd2])
        K.dma("sync", d["dbg_scs"][:], scs[:], "zDBG", wait=[sg_pt])
    K.barrier()
    K.free("A_scs", "A_stok", "A_m1t", "A_m2t", "A_msk", "A_mski", "A_m1b",
           "A_rtok", "A_wqt", "A_ones16", "A_scraw", "routeT",
           "A_xf0", "A_xf1", "A_xnf", "A_nmp", "A_nmb", "A_xnb", "A_qk16",
           "A_lkT", "A_mr30", "A_mr31",
           "A_psb", "A_pssc", "A_psqk", "A_pstr")

def phase_L0(K, d, wp, st):
    idf = K.store["c_idf"][1]
    onesf = K.store["c_onesf"][1]
    qqT = K.store["qqT"][1]
    qkT2 = K.store["qkT2"][1]
    lqT = K.store["lqT"][1]
    ltok = K.store["ltok"][1]

    bhg = K.sbuf("L_bhg", (128, 16), F32)
    bo = K.sbuf("L_bo", (128, 8), F32)
    bhv = K.sbuf("L_bhv", (1, HID), BF16)
    onesrb = K.sbuf("L_onesrb", (1, 128), BF16)
    braw = K.sbuf("L_braw", (128, 2, 256), F32)
    bsc = K.sbuf("L_bsc", (128, 512), F32)
    linkv = K.sbuf("L_linkv", (128, HID), BF16)
    K.dma("sync", bhg[:], d["bhg"][:], "lCST")
    K.dma("sync", bo[:], d["bo"][:], "lCST")
    K.dma("sync", bhv[:], d["s_bhv"][:], "lCST", wait=[wp["bhv"]])
    lcst_pt = K.dma("sync", braw[:], d["biasT"][:], "lCST")
    bo_pt = lcst_pt
    bhv_pt = lcst_pt
    braw_pt = lcst_pt
    ob_pt = K.op("vector", "memset", onesrb[:], 1.0, inc=True)
    bsc_pt = K.op("scalar", "mul", wait=[braw_pt], inc=True,
                  out=bsc[:].rearrange("p (j i) -> p j i", j=2), in_=braw[:],
                  mul=float(GSZ * math.sqrt(QKD)))

    PP = K.psum("L_PP", (128, 2048), F32)
    PQ = K.psum("L_PQ", (128, 2048), F32)

    nmb_pct = d["s_nmb"][:].rearrange("c p t -> p c t")
    xT_pct = d["xT"][:].rearrange("c p t -> p c t")
    x1_pct = d["s_x1"][:].rearrange("c p t -> p c t")

    whv = K.sbuf("L_whv", (128, CC, HID), BF16)
    whg = K.sbuf("L_whg", (128, CC, 16, 128), BF16)
    wo = K.sbuf("L_wo", (128, 16, CC, 128), BF16)
    whv_pt = K.dma("sync", whv[:], d["s_whv"][:].rearrange("c p t -> p c t"),
                   "lWHV", wait=[wp["whv"]])
    K.dma("sync", whg[:], d["s_whg"][:], "lWGO", wait=[wp["whg"]])
    wgo_pt = K.dma("sync", wo[:], d["s_wo"][:], "lWGO", wait=[wp["wo"]])
    whg_pt = wgo_pt
    wo_pt = wgo_pt
    for b in range(NB):
        nmch = [K.sbuf(f"L_nmch{b}_{i}", (128, CC, TC), BF16) for i in range(2)]
        vt = [K.sbuf(f"L_vt{b}_{i}", (128, HID), BF16) for i in range(2)]
        ev_v = [None, None]
        vt_used = [None, None]
        ld_nm = [None, None]
        nm_used = [None, None]
        kvmm = None
        for ch in range(4):
            cb = ch % 2
            gtsl = slice(b * S + ch * TC, b * S + (ch + 1) * TC)
            ld_nm[cb] = K.dma("sync", nmch[cb][:], nmb_pct[:, :, gtsl],
                              f"lNM{cb}", wait=[nm_used[cb], whv_pt, bhv_pt,
                                                ("aSTn", K.cnt["aSTn"])])
            for tt in range(4):
                t_loc = ch * 4 + tt
                tb = t_loc % 2
                for fc in range(4):
                    par = fc % 2
                    reg = PQ[:, par * 512:(par + 1) * 512]
                    mmv = None
                    for i in range(CC):
                        mmv = K.mm(reg, nmch[cb][:, i, tt * 128:(tt + 1) * 128],
                                   whv[:, i, fc * 512:(fc + 1) * 512],
                                   start=(i == 0), stop=False,
                                   wait=(ld_nm[cb], ev_v[par], vt_used[tb]))
                    mmv = K.mm(reg, onesrb[:], bhv[0:1, fc * 512:(fc + 1) * 512],
                               start=False, stop=True, wait=[ob_pt], inc=True)
                    ev_v[par] = K.act(vt[tb][:, fc * 512:(fc + 1) * 512], reg,
                                      AF.Silu, wait=[mmv, vt_used[tb]], inc=True)
                gt = b * 16 + t_loc
                for ec in range(4):
                    kvmm = K.mm(PP[:, ec * 512:(ec + 1) * 512],
                                ltok[:, gt * 128:(gt + 1) * 128],
                                vt[tb][:, ec * 512:(ec + 1) * 512],
                                start=(t_loc == 0), stop=(t_loc == 15),
                                wait=[ev_v[0], ev_v[1]],
                                inc=(t_loc == 15 and ec == 3))
                stv = K.dma("sync", d["s_vt"][b, t_loc], vt[tb][:],
                            f"lVT{b}_{tb}", wait=[ev_v[0], ev_v[1]])
                vt_used[tb] = stv
            nm_used[cb] = ("TE", K.cnt["TE"])
        kv_pt = K.act(linkv[:], PP[:, 0:2048], AF.Identity, scale=1.0 / S,
                      wait=[kvmm], inc=True)
        K.barrier()
        K.free(f"L_nmch{b}_0", f"L_nmch{b}_1",
               f"L_vt{b}_0", f"L_vt{b}_1")

        gate16 = K.sbuf(f"L_gate16{b}", (128, 16, 256), BF16)
        attnT = K.sbuf(f"L_attnT{b}", (128, 512), BF16)
        afp = K.sbuf(f"L_afp{b}", (128, 512), F32)
        nmg = [K.sbuf(f"L_nmg{b}_{i}", (128, CC, 256), BF16) for i in range(2)]
        xfg = [K.sbuf(f"L_xfg{b}_{i}", (128, CC, 256), F32) for i in range(2)]
        mrow = [K.sbuf(f"L_mrow{b}_{i}", (1, 512), F32) for i in range(2)]
        vg = [K.sbuf(f"L_vg{b}_{i}", (128, 2, HID), BF16) for i in range(2)]
        xng = K.sbuf(f"L_xng{b}", (128, CC, 256), F32)

        ld_g = [None, None]
        g_used = [None, None]
        ev_sim = None
        ev_gate = None
        at_pt = None
        ev_ql = None
        ev_wo = None
        ev_bc = None
        st_x1 = None
        for gi in range(8):
            gb = gi % 2
            gg = b * 8 + gi
            gcols = slice(gg * 256, (gg + 1) * 256)
            if gi == 0:
                astf = [("aSTn", K.cnt["aSTn"]), ("aSTm0", K.cnt["aSTm0"]),
                        ("aSTm1", K.cnt["aSTm1"])]
                K.dma("sync", nmg[0][:], nmb_pct[:, :, gcols], f"lGR0_{b}", wait=astf)
                K.dma("sync", xfg[0][:], xT_pct[:, :, gcols], f"lGR0_{b}")
                K.dma("sync", mrow[0][0:1, 0:256], d["s_mr"][0:1, gcols], f"lGR0_{b}")
                g0pt = K.dma("sync", mrow[0][0:1, 256:512], d["s_mr"][1:2, gcols], f"lGR0_{b}")
                v0pt = K.dma("sync", vg[0][:],
                             d["s_vt"][b, 0:2].rearrange("j p f -> p j f"),
                             f"lVG0_{b}",
                             wait=[(f"lVT{b}_0", K.cnt[f"lVT{b}_0"]),
                                   (f"lVT{b}_1", K.cnt[f"lVT{b}_1"])])
                ld_g[0] = [g0pt, g0pt, g0pt, g0pt, v0pt]
            if gi + 1 < 8:
                nx = slice((gg + 1) * 256, (gg + 2) * 256)
                nb_ = (gi + 1) % 2
                w_ = [g_used[nb_], st_x1]
                K.dma("sync", nmg[nb_][:], nmb_pct[:, :, nx], f"lGR{nb_}_{b}", wait=w_)
                K.dma("sync", xfg[nb_][:], xT_pct[:, :, nx], f"lGR{nb_}_{b}")
                K.dma("sync", mrow[nb_][0:1, 0:256], d["s_mr"][0:1, nx], f"lGR{nb_}_{b}")
                gpt = K.dma("sync", mrow[nb_][0:1, 256:512], d["s_mr"][1:2, nx], f"lGR{nb_}_{b}")
                vpt = K.dma("sync", vg[nb_][:],
                            d["s_vt"][b, 2 * (gi + 1):2 * (gi + 1) + 2].rearrange("j p f -> p j f"),
                            f"lVG{nb_}_{b}", wait=w_)
                ld_g[nb_] = [gpt, gpt, gpt, gpt, vpt]
            mms = None
            for j in range(2):
                mms = K.mm(PQ[:, j * 256:(j + 1) * 256],
                           qkT2[:, (gg * 2 + j) * 128:(gg * 2 + j + 1) * 128],
                           qqT[:, gcols], start=(j == 0), stop=(j == 1),
                           wait=[ev_sim], inc=(j == 1))
            ev_sim = K.op("vector", "tensor_add", wait=[mms, bsc_pt, at_pt], inc=True,
                          out=afp[:], in0=bsc[:], in1=PQ[:, 0:512])
            rl = K.act(afp[:], afp[:], AF.Relu, wait=[ev_sim], inc=True)
            at_pt = K.act(attnT[:], afp[:], AF.Square, scale=1.0 / GSZ, inc=True)
            for rnd in range(2):
                for fi in range(8):
                    ft = rnd * 8 + fi
                    reg = PP[:, fi * 256:(fi + 1) * 256]
                    for i in range(CC):
                        mmg = K.mm(reg, whg[:, i, ft, :], nmg[gb][:, i, :],
                                   start=(i == 0 and fi % 2 == 0),
                                   stop=(i == CC - 1 and fi % 2 == 1),
                                   wait=(ld_g[gb][0], whg_pt, ev_gate, ev_wo),
                                   inc=(fi == 7 and i == CC - 1))
                for fi in range(8):
                    ft = rnd * 8 + fi
                    ev_gate = K.act(gate16[:, ft, :], PP[:, fi * 256:(fi + 1) * 256],
                                    AF.Silu, bias=bhg[:, ft:ft + 1],
                                    wait=[mmg], inc=True)
            for rnd in range(4):
                mmq = None
                for ei in range(4):
                    et = rnd * 4 + ei
                    reg = PQ[:, 512 + ei * 256:512 + (ei + 1) * 256]
                    mmq = K.mm(reg, vg[gb][:, 0, et * 128:(et + 1) * 128],
                               attnT[:, 0:256], start=(ei % 2 == 0), stop=False,
                               wait=(at_pt, ev_ql, kv_pt, ld_g[gb][4]))
                    mmq = K.mm(reg, vg[gb][:, 1, et * 128:(et + 1) * 128],
                               attnT[:, 256:512], start=False, stop=False)
                    mmq = K.mm(reg, linkv[:, et * 128:(et + 1) * 128],
                               lqT[:, gcols], start=False, stop=(ei % 2 == 1),
                               inc=(ei == 3))
                ev_ql = K.op("vector", "tensor_mul", wait=[mmq, ev_gate], inc=True,
                             out=gate16[:, rnd * 4:(rnd + 1) * 4, :],
                             in0=gate16[:, rnd * 4:(rnd + 1) * 4, :],
                             in1=PQ[:, 512:1536].rearrange("p (e i) -> p e i", e=4))
            mmw = None
            for dt in range(8):
                reg = PP[:, (dt // 2) * 512 + (dt % 2) * 256:
                          (dt // 2) * 512 + (dt % 2 + 1) * 256]
                for ec in range(16):
                    mmw = K.mm(reg, wo[:, ec, dt, :], gate16[:, ec, :],
                               start=(ec == 0 and dt % 2 == 0),
                               stop=(ec == 15 and dt % 2 == 1),
                               wait=(ev_ql, wo_pt, ev_wo, ev_gate),
                               inc=(dt == 7 and ec == 15))
            mmb = K.mm(PQ[:, 1536:1792], onesf[:], mrow[gb][0:1, 0:256], True, False,
                       wait=[ld_g[gb][0], ev_bc])
            mmb = K.mm(PQ[:, 1792:2048], onesf[:], mrow[gb][0:1, 256:512], False, True,
                       inc=True)
            mbv = PQ[:, 1536:1792][:, None, :].broadcast_to((128, CC, 256))
            rbv = PQ[:, 1792:2048][:, None, :].broadcast_to((128, CC, 256))
            K.op("vector", "tensor_sub", wait=[mmb, ld_g[gb][1]],
                 out=xng[:], in0=xfg[gb][:], in1=mbv)
            ev_bc = K.op("vector", "tensor_mul", inc=True,
                         out=xng[:], in0=xng[:], in1=rbv)
            gx = K.op("vector", "tensor_add", inc=True,
                      out=xng[:], in0=xng[:], in1=xfg[gb][:])
            oo = None
            for dt in range(8):
                reg = PP[:, (dt // 2) * 512 + (dt % 2) * 256:
                          (dt // 2) * 512 + (dt % 2 + 1) * 256]
                oo = K.act(xfg[gb][:, dt, :], reg, AF.Identity, bias=bo[:, dt:dt + 1],
                           wait=[mmw, bo_pt, st_x1, gx], inc=True)
            ev_wo = oo
            gu = K.op("vector", "tensor_add", wait=[oo], inc=True,
                      out=xfg[gb][:], in0=xfg[gb][:], in1=xng[:])
            g_used[gb] = gu
            st_x1 = K.dma("sync", x1_pct[:, :, gcols], xfg[gb][:], "lSX", wait=[gu])
        K.barrier()
        K.free(f"L_gate16{b}",
               f"L_attnT{b}", f"L_afp{b}", f"L_nmg{b}_0", f"L_nmg{b}_1",
               f"L_xfg{b}_0", f"L_xfg{b}_1", f"L_mrow{b}_0", f"L_mrow{b}_1",
               f"L_vg{b}_0", f"L_vg{b}_1", f"L_xng{b}")
    if DEV:
        K.dma("sync", d["dbg_x1"][:], d["s_x1"][:], "zDBG",
              wait=[("lSX", K.cnt["lSX"])])
    K.barrier()
    K.free("L_whv", "L_whg", "L_wo",
           "L_bhg", "L_bo", "L_bhv", "L_onesrb", "L_braw", "L_bsc", "L_linkv",
           "L_PP", "L_PQ", "qqT", "qkT2", "lqT", "ltok")
    return {}

def phase_B(K, d, wp, st):
    onesf = K.store["c_onesf"][1]

    acc = K.sbuf("B_acc", (128, CC, S), F32)
    xnh = K.sbuf("B_xnh", (128, CC, S), BF16)
    wg = [K.sbuf(f"B_wg{i}", (128, 4, CC, 128), BF16) for i in range(2)]
    wu = [K.sbuf(f"B_wu{i}", (128, 4, CC, 128), BF16) for i in range(2)]
    wd = [K.sbuf(f"B_wd{i}", (128, CC, 4, 128), BF16) for i in range(2)]
    rrow = [K.sbuf(f"B_rrow{i}", (1, S), F32) for i in range(2)]
    sg16 = K.sbuf("B_sg16", (128, 4, 512), BF16)
    sb16 = K.sbuf("B_sb16", (128, CC, 512), BF16)
    xnb1 = K.sbuf("B_xnb1", (128, CC, 512), BF16)
    xtm = K.sbuf("B_xtm", (128, CC, 512), BF16)
    brow = K.sbuf("B_brow", (1, 1024), F32)
    brow2 = K.sbuf("B_brow2", (1, 1024), F32)
    ceps2 = K.store["c_eps"][1]
    onesb = K.store["c_onesb"][1]
    hr16 = K.sbuf("B_hr16", (128, 4, 512), BF16)
    PG = K.psum("B_PG", (128, 2048), F32)
    PU = K.psum("B_PU", (128, 2048), F32)

    x1_pct = d["s_x1"][:].rearrange("c p t -> p c t")
    xnb_pct = d["s_xnb"][:].rearrange("c p t -> p c t")

    NEXP = NE + 1
    wld = [None, None]
    wused = [None, None]
    st_acc = None
    for half in range(NB):
        hsl = slice(half * S, (half + 1) * S)
        K.dma("sync", acc[:], x1_pct[:, :, hsl], f"bAC{half}",
              wait=[st_acc, ("lSX", K.cnt["lSX"])])
        ld_pt = K.dma("sync", xnh[:], xnb_pct[:, :, hsl], f"bAC{half}",
                      wait=[("TE", K.cnt["TE"]), ("aSTx", K.cnt["aSTx"])])
        ld_acc = ld_pt
        ld_xnh = ld_pt
        evA = None
        evB = None
        evAt = [None] * 4
        evBt = [None] * 4
        ev_sg = None
        sgr_prev = None
        hr_pt = None
        first = (half == 0)
        for e in range(NEXP):
            eb = e % 2
            if half == 0 or e >= 2:
                pass
            srcg = d["s_mwg"][e] if e < NE else d["s_wsg"][:]
            srcu = d["s_mwu"][e] if e < NE else d["s_wsu"][:]
            srcd = d["s_mwd"][e] if e < NE else d["s_wsd"][:]
            wprep = wp[f"moe{e}"] if e < NE else wp["moeS"]
            K.dma("sync", wg[eb][:], srcg, f"bWL{eb}", wait=[wused[eb], wprep])
            K.dma("sync", wu[eb][:], srcu, f"bWL{eb}", wait=[wprep])
            l3 = K.dma("sync", wd[eb][:], srcd, f"bWL{eb}", wait=[wprep])
            if e < NE:
                l3 = K.dma("sync", rrow[eb][0:1, :], d["s_route"][e:e + 1, hsl],
                           f"bWL{eb}", wait=[("aSTr", K.cnt["aSTr"])])
            wld[eb] = l3
            lr = l3
            for ch in range(4):
                csl = slice(ch * 512, (ch + 1) * 512)
                gmm = None
                for mt in range(4):
                    for i in range(CC):
                        gmm = K.mm(PG[:, mt * 512:(mt + 1) * 512],
                                   wg[eb][:, mt, i, :], xnh[:, i, csl],
                                   start=(i == 0), stop=(i == CC - 1),
                                   wait=(ld_xnh, wld[eb], evBt[mt]),
                                   inc=(mt == 3 and i == CC - 1))
                ev_sg = K.act(sg16[:], PG[:, 0:2048].rearrange("p (m t) -> p m t", m=4),
                              AF.Silu, wait=[gmm], inc=True)
                if e < NE:
                    rmm = K.mm(PG[:, 0:512], onesf[:], rrow[eb][0:1, csl],
                               True, True, wait=[ev_sg, lr], inc=True)
                    sgr_pt = K.op("vector", "tensor_mul", wait=[rmm], inc=True,
                                  out=sg16[:], in0=sg16[:],
                                  in1=PG[:, 0:512][:, None, :].broadcast_to((128, 4, 512)))
                else:
                    sgr_pt = ev_sg
                ummt = [None] * 4
                for mt in range(4):
                    for i in range(CC):
                        ummt[mt] = K.mm(PU[:, mt * 512:(mt + 1) * 512],
                                        wu[eb][:, mt, i, :], xnh[:, i, csl],
                                        start=(i == 0), stop=(i == CC - 1),
                                        wait=(evAt[mt],), inc=(i == CC - 1))
                hr_pt = None
                for mt in range(4):
                    hr_pt = K.op("vector", "tensor_mul", wait=[ummt[mt]],
                                 swait=[sgr_pt], inc=True,
                                 out=hr16[:, mt, :], in0=sg16[:, mt, :],
                                 in1=PU[:, mt * 512:(mt + 1) * 512])
                hsrc = hr16
                for dt in range(4):
                    dA = None
                    for mc in range(4):
                        dA = K.mm(PU[:, dt * 512:(dt + 1) * 512],
                                  wd[eb][:, dt, mc, :], hsrc[:, mc, :],
                                  start=(mc == 0), stop=(mc == 3),
                                  wait=(hr_pt,), inc=(mc == 3))
                    evAt[dt] = K.op("vector", "tensor_add", wait=[dA, ld_acc],
                                    swait=[hr_pt], inc=True,
                                    out=acc[:, dt, csl], in0=acc[:, dt, csl],
                                    in1=PU[:, dt * 512:(dt + 1) * 512])
                evA = evAt[3]
                for dt in range(4):
                    dB = None
                    for mc in range(4):
                        dB = K.mm(PG[:, dt * 512:(dt + 1) * 512],
                                  wd[eb][:, 4 + dt, mc, :], hsrc[:, mc, :],
                                  start=(mc == 0), stop=(mc == 3),
                                  wait=(hr_pt, sgr_pt), inc=(mc == 3))
                    evBt[dt] = K.op("vector", "tensor_add", wait=[dB], inc=True,
                                    out=acc[:, 4 + dt, csl], in0=acc[:, 4 + dt, csl],
                                    in1=PG[:, dt * 512:(dt + 1) * 512])
                evB = evBt[3]
                sgr_prev = sgr_pt
            wused[eb] = ("TE", K.cnt["TE"])
        st_acc = K.dma("sync", x1_pct[:, :, hsl], acc[:], "bSX", wait=[evA, evB])
        ev2 = None
        xn1pt = None
        for ch in range(4):
            csl = slice(ch * 512, (ch + 1) * 512)
            tsl2 = slice(half * S + ch * 512, half * S + (ch + 1) * 512)
            cpt = K.op("vector", "tensor_copy", wait=[evA, evB, xn1pt], inc=True,
                       out=sb16[:], in_=acc[:, :, csl])
            mmx = None
            for i in range(CC):
                mmx = K.mm(PG[0:1, 0:512], onesb[:], sb16[:, i, :], start=(i == 0),
                           stop=(i == CC - 1), wait=[cpt, ev2], inc=(i == CC - 1))
            sqt = K.act(sb16[:], acc[:, :, csl], AF.Square, wait=[evA, evB, mmx],
                        inc=True)
            mmst = None
            for i in range(CC):
                mmst = K.mm(PG[0:1, 512:1024], onesb[:], sb16[:, i, :], start=(i == 0),
                            stop=(i == CC - 1), wait=[sqt], inc=(i == CC - 1))
            K.act(brow[0:1, 0:512], PG[0:1, 0:512], AF.Identity, scale=1.0 / D,
                  wait=[mmst])
            ev2 = K.act(brow[0:1, 512:1024], PG[0:1, 512:1024], AF.Identity,
                        scale=1.0 / D, inc=True)
            K.dma("sync", d["s_mr1"][0:1, tsl2], brow[0:1, 0:512], "cSTm", wait=[ev2])
            K.op("vector", "tensor_mul", wait=[ev2], out=brow2[0:1, 0:512],
                 in0=brow[0:1, 0:512], in1=brow[0:1, 0:512])
            vv = K.op("vector", "tensor_sub", inc=True, out=brow2[0:1, 0:512],
                      in0=brow[0:1, 512:1024], in1=brow2[0:1, 0:512])
            sdp = K.act(brow2[0:1, 512:1024], brow2[0:1, 0:512], AF.Sqrt,
                        bias=ceps2[0:1, :], wait=[vv], inc=True)
            rrp = K.op("vector", "reciprocal", wait=[sdp], inc=True,
                       out=brow2[0:1, 0:512], in_=brow2[0:1, 512:1024])
            K.dma("sync", d["s_mr1"][1:2, tsl2], brow2[0:1, 0:512], "cSTm", wait=[rrp])
            K.mm(PU[:, 0:512], onesf[:], brow[0:1, 0:512], True, True,
                 wait=[rrp, xn1pt])
            bcp = K.mm(PU[:, 512:1024], onesf[:], brow2[0:1, 0:512], True, True,
                       inc=True)
            mbv2 = PU[:, 0:512][:, None, :].broadcast_to((128, CC, 512))
            rbv2 = PU[:, 512:1024][:, None, :].broadcast_to((128, CC, 512))
            K.op("vector", "tensor_sub", wait=[bcp], out=xtm[:],
                 in0=acc[:, :, csl], in1=mbv2)
            xn1pt = K.op("vector", "tensor_mul", inc=True, out=xnb1[:],
                         in0=xtm[:], in1=rbv2)
            K.dma("sync", d["s_xn1b"][:].rearrange("c p t -> p c t")[:, :, tsl2],
                  xnb1[:], "cSTx", wait=[xn1pt])
    if DEV:
        K.dma("sync", d["dbg_x1f"][:], d["s_x1"][:], "zDBG",
              wait=[("bSX", K.cnt["bSX"])])
    K.barrier()
    K.free("B_acc", "B_xnh", "B_wg0", "B_wg1", "B_wu0", "B_wu1", "B_wd0",
           "B_wd1", "B_rrow0", "B_rrow1", "B_sg16", "B_sb16",
           "B_xnb1", "B_xtm", "B_brow", "B_brow2",
           "B_hr16", "B_PG", "B_PU")
    return {}

def phase_C(K, d, wp, st):
    onesf = K.store["c_onesf"][1]
    onesb = K.store["c_onesb"][1]
    idf = K.store["c_idf"][1]
    ceps = K.store["c_eps"][1]

    x1_pct = d["s_x1"][:].rearrange("c p t -> p c t")
    xn1_pct = d["s_xn1b"][:].rearrange("c p t -> p c t")
    aout_pct = d["s_aout"][:].rearrange("c p t -> p c t")
    out_pct = d["out"][:].rearrange("c p t -> p c t")

    bsx = ("bSX", K.cnt["bSX"])

    mbuf = K.sbuf("C_mbuf", (128, 16), F32)
    m1t = K.sbuf("C_m1t", (128, 1), F32)
    cm44 = K.sbuf("C_cm44", (1, 1), F32)
    K.op("vector", "memset", cm44[:], -44.0)
    mrow1 = K.sbuf("C_mrow1", (1, 128), F32)
    negc = K.sbuf("C_negc", (1, 1), F32)
    cbias = K.sbuf("C_cbias", (128, 1), F32)
    aob = K.sbuf("C_aob", (128, CC, 512), F32)
    for b in range(NB):
        hsl = slice(b * S, (b + 1) * S)
        q16 = K.sbuf(f"C_q16{b}", (128, CC, S), BF16)
        k16 = K.sbuf(f"C_k16{b}", (128, CC, S), BF16)
        vaug = K.sbuf(f"C_vaug{b}", (128, 16, 16, 65), BF16)
        xn1h = K.sbuf(f"C_xn1h{b}", (128, CC, S), BF16)
        wq1 = K.sbuf(f"C_wq1{b}", (128, CC, CC, 128), BF16)
        wk1 = K.sbuf(f"C_wk1{b}", (128, CC, CC, 128), BF16)
        wv1 = K.sbuf(f"C_wv1{b}", (128, CC, 1024), BF16)
        CP = K.psum(f"C_CP{b}", (128, 2048), F32)
        ldx = K.dma("sync", xn1h[:], xn1_pct[:, :, hsl], f"cXN{b}",
                    wait=[("cSTx", K.cnt["cSTx"])])
        K.dma("gpsimd", wq1[:], d["s_wq1"][:], f"cWQ{b}", wait=[wp["PREP"]])
        wqk_pt = K.dma("gpsimd", wk1[:], d["s_wk1"][:], f"cWQ{b}")
        wl_pt = K.dma("sync", wv1[:], d["s_wv1"][:].rearrange("c p f -> p c f"),
                      f"cWV{b}")
        on_pt = K.op("vector", "memset", vaug[:, :, :, 64:65], 1.0, inc=True)
        evq = [None] * 4
        ev_q = None
        mmq = None
        rix = 0
        for dst, w, lab in ((q16, wq1, 0), (k16, wk1, 1)):
            for ft in range(CC):
                for tc in range(4):
                    reg = CP[:, (rix % 4) * 512:(rix % 4 + 1) * 512]
                    for i in range(CC):
                        mmq = K.mm(reg, w[:, i, ft, :], xn1h[:, i, tc * 512:(tc + 1) * 512],
                                   start=(i == 0), stop=(i == CC - 1),
                                   wait=(ldx, wl_pt, wqk_pt, evq[rix % 4]),
                                   inc=(i == CC - 1))
                    evq[rix % 4] = K.op("vector", "tensor_copy",
                                        out=dst[:, ft, tc * 512:(tc + 1) * 512], in_=reg,
                                        wait=[mmq], inc=True)
                    rix += 1
        for tt in range(16):
            for fc in range(2):
                reg = CP[:, (rix % 4) * 512:(rix % 4 + 1) * 512]
                for i in range(CC):
                    mmq = K.mm(reg, xn1h[:, i, tt * 128:(tt + 1) * 128],
                               wv1[:, i, fc * 512:(fc + 1) * 512],
                               start=(i == 0), stop=(i == CC - 1),
                               wait=(evq[rix % 4],), inc=(i == CC - 1))
                evq[rix % 4] = K.op("vector", "tensor_copy", wait=[mmq, on_pt],
                                    out=vaug[:, tt, fc * 8:(fc + 1) * 8, 0:64],
                                    in_=reg.rearrange("p (h e) -> p h e", h=8), inc=True)
                rix += 1
        ev_q = evq[(rix - 1) % 4]
        qkv_done = ("VE", K.cnt["VE"])
        K.barrier()
        K.free(f"C_wq1{b}", f"C_wk1{b}", f"C_wv1{b}", f"C_xn1h{b}")
        wo1 = K.sbuf(f"C_wo1{b}", (128, CC, CC, 128), BF16)
        expb = K.sbuf(f"C_expb{b}", (128, 16, 512), BF16)
        avc = K.sbuf(f"C_avc{b}", (128, 4, CC, 512), BF16)
        wo1_pt = K.dma("gpsimd", wo1[:], d["s_wo1"][:], f"cWO{b}", wait=[wp["PREP"]])
        CAV = K.psum(f"C_CAV{b}", (128, 512), F32)

        kaug = [K.sbuf(f"C_kaug{b}_{i}", (128, S), BF16) for i in range(2)]
        qaug = [K.sbuf(f"C_qaug{b}_{i}", (128, S), BF16) for i in range(2)]
        kz = [K.sbuf(f"C_kz{b}_{i}", (128, 512), BF16) for i in range(2)]
        zrow = [K.sbuf(f"C_zrow{b}_{i}", (1, 512), F32) for i in range(2)]
        ivbb = [K.sbuf(f"C_ivbb{b}_{i}", (64, 512), F32) for i in range(2)]
        mtmp = [K.sbuf(f"C_mtmp{b}_{i}", (16, 128), BF16) for i in range(2)]
        mbufs = [K.sbuf(f"C_mbuf2{b}_{i}", (128, 16), F32) for i in range(2)]
        P1T = K.psum(f"C_P1T{b}", (128, 1024), F32)
        CAV2 = K.psum(f"C_CAV2{b}", (128, 512), F32)
        rd_p1 = [None, None]
        ev_mt = [None, None]
        crow_pt = [None, None]
        p1ld = [None, None]
        ev_exp_reg = [None] * 4
        rd_av_reg = [None, None]
        rec_prev = [None, None]
        ibm_prev = [None, None]
        st_ao = None
        wo_mm = None
        ev_wv = None
        aug_used = [None, None]
        tr_used = None
        km = [K.op("vector", "memset", kaug[i][64:128, :], 0.0) for i in range(2)]
        qm = [K.op("vector", "memset", qaug[i][64:128, :], 0.0) for i in range(2)]
        kzm = [K.op("vector", "memset", kz[i][64:128, :], 0.0) for i in range(2)]
        kon = [K.dma("sync", kaug[i][64:65, :], d["onesr2k"][:], f"cAG{b}_{i}",
                     wait=[km[i]]) for i in range(2)]

        def pass1_load(h):
            hb = h % 2
            ft = h // 2
            pb = 64 * (h % 2)
            ck = K.dma("sync", kaug[hb][0:64, :], k16[pb:pb + 64, ft, :],
                       f"cAG{b}_{hb}", wait=[qkv_done, aug_used[hb], qm[hb], kzm[hb]])
            ckz = K.dma("sync", kz[hb][0:64, 0:128], k16[pb:pb + 64, ft, 0:128],
                        f"cAG{b}_{hb}")
            p1ld[hb] = K.dma("sync", qaug[hb][0:64, :], q16[pb:pb + 64, ft, :],
                             f"cAG{b}_{hb}")

        def pass1_step(h, step):
            hb = h % 2
            for it in range(step * 4, step * 4 + 4):
                bk = it % 2
                mm1 = K.mm(P1T[:, bk * 512:bk * 512 + 128],
                           qaug[hb][:, it * 128:(it + 1) * 128],
                           kz[hb][:, 0:128], start=True, stop=True,
                           wait=(p1ld[hb], rd_p1[bk], tr_used,
                                 ev_mt[0], ev_mt[1]), inc=True)
                rd_p1[bk] = K.op("vector", "reduce_max", wait=[mm1, ev_mt[hb]],
                                 out=mbufs[hb][:, it:it + 1],
                                 in_=P1T[:, bk * 512:bk * 512 + 128],
                                 axis=AX.X)

        def pass1_fin(h):
            nonlocal tr_used
            hb = h % 2
            tp = K.tr(P1T[0:16, 0:128], mbufs[hb][:], idf[:],
                      wait=[rd_p1[0], rd_p1[1], ev_mt[hb]], inc=True)
            tr_used = tp
            ev_mt[hb] = K.op("scalar", "mul", wait=[tp], out=mtmp[hb][:],
                             in_=P1T[0:16, 0:128], mul=-1.0, inc=True)
            c1 = K.dma("sync", d["s_crow"][hb], mtmp[hb][:], f"cAG{b}_{hb}",
                       wait=[ev_mt[hb]])
            crow_pt[hb] = K.dma("sync", qaug[hb][64:65, 0:S],
                                d["s_crow"][hb].rearrange("a i -> (a i)")[None, :],
                                f"cAG{b}_{hb}", wait=[c1])

        def pass2_emit(h, p1h):
            nonlocal aug_used
            hb = h % 2
            ft = h // 2
            pb = 64 * (h % 2)
            for ic in range(4):
                if p1h is not None and ic < 2:
                    pass1_step(p1h, 2 * ic)
                    pass1_step(p1h, 2 * ic + 1)
                if p1h is not None and ic == 2:
                    pass1_fin(p1h)
                icp = ic % 2
                isl = slice(ic * 512, (ic + 1) * 512)
                AVr = CAV if icp == 0 else CAV2
                avm = None
                exp_pts = [None] * 16
                for p in range(8):
                    mme = None
                    for jj in range(2):
                        jt = 2 * p + jj
                        reg = CP[:, (jt % 4) * 512:(jt % 4 + 1) * 512]
                        mme = K.mm(reg,
                                   kaug[hb][:, jt * 128:(jt + 1) * 128],
                                   qaug[hb][:, isl], start=True, stop=True,
                                   wait=(crow_pt[hb], kon[hb], ev_exp_reg[p % 2]),
                                   inc=(jj == 1))
                    ev_exp_reg[p % 2] = K.act(
                        expb[:, 2 * p:2 * p + 2, :].rearrange("p j i -> p (j i)"),
                        CP[:, (p % 2) * 1024:(p % 2 + 1) * 1024],
                        AF.Exp, scale=L1SCALE, wait=[mme], inc=True)
                    exp_pts[2 * p] = exp_pts[2 * p + 1] = ev_exp_reg[p % 2]
                for jv in range(16):
                    avm = K.mm(AVr[0:65, :], vaug[:, jv, h, :], expb[:, jv, :],
                               start=(jv == 0), stop=(jv == 15),
                               wait=(exp_pts[jv], rd_av_reg[icp],
                                     rec_prev[icp]),
                               inc=(jv == 15))
                zc = K.op("scalar", "copy", wait=[avm], swait=[ibm_prev[icp]],
                          out=zrow[icp][:], in_=AVr[64:65, :])
                ibm = K.mm(AVr[64:128, :], onesf[0:1, 0:64], zrow[icp][:],
                           True, True, wait=[zc, rec_prev[icp]], inc=True)
                ibm_prev[icp] = ibm
                rec = K.op("vector", "reciprocal", wait=[ibm], inc=True,
                           out=ivbb[icp][:], in_=AVr[64:128, :])
                rec_prev[icp] = rec
                rd_av_reg[icp] = K.op("vector", "tensor_mul", wait=[rec],
                                      inc=True,
                                      out=avc[pb:pb + 64, ic, ft, :],
                                      in0=AVr[0:64, :], in1=ivbb[icp][:])
            aug_used[hb] = ("TE", K.cnt["TE"])

        pass1_load(0)
        for st in range(4):
            pass1_step(0, st)
        pass1_fin(0)
        for h in range(1, NHEAD + 1):
            if h < NHEAD:
                pass1_load(h)
            pass2_emit(h - 1, h if h < NHEAD else None)
        for ic in range(4):
            for wave in range(2):
                wo_mm = None
                for dtw in range(4):
                    dt = wave * 4 + dtw
                    reg = CP[:, dtw * 512:(dtw + 1) * 512]
                    for ec in range(CC):
                        wo_mm = K.mm(reg, wo1[:, ec, dt, :], avc[:, ic, ec, :],
                                     start=(ec == 0), stop=(ec == CC - 1),
                                     wait=(rd_av_reg[0], rd_av_reg[1], wo1_pt,
                                           ev_exp_reg[0], ev_exp_reg[1],
                                           ev_exp_reg[2], ev_exp_reg[3],
                                           rd_p1[0], rd_p1[1], ev_wv),
                                     inc=(dtw == 3 and ec == CC - 1))
                for dtw in range(4):
                    dt = wave * 4 + dtw
                    ev_wv = K.act(aob[:, dt, :], CP[:, dtw * 512:(dtw + 1) * 512],
                                  AF.Identity, wait=[wo_mm, st_ao], inc=True)
            st_ao = K.dma("sync", aout_pct[:, :, b * S + ic * 512:b * S + (ic + 1) * 512],
                          aob[:], f"cAOE{b}", wait=[ev_wv])
        K.barrier()
        K.free(f"C_q16{b}", f"C_k16{b}", f"C_vaug{b}", f"C_CP{b}",
               f"C_wo1{b}", f"C_expb{b}", f"C_avc{b}", f"C_CAV{b}",
               f"C_kaug{b}_0", f"C_kaug{b}_1", f"C_qaug{b}_0", f"C_qaug{b}_1",
               f"C_kz{b}_0", f"C_kz{b}_1", f"C_zrow{b}_0", f"C_zrow{b}_1",
               f"C_ivbb{b}_0", f"C_ivbb{b}_1",
               f"C_mtmp{b}_0", f"C_mtmp{b}_1", f"C_mbuf2{b}_0", f"C_mbuf2{b}_1",
               f"C_P1T{b}", f"C_CAV2{b}")
    K.free("C_mbuf", "C_m1t", "C_cm44", "C_mrow1", "C_negc", "C_cbias",
           "C_aob")
    K.barrier()

    gam1 = K.sbuf("C2_gam1", (128, 8), F32)
    bet1 = K.sbuf("C2_bet1", (128, 8), F32)
    x1c = [K.sbuf(f"C2_x1c{i}", (128, CC, TC), F32) for i in range(2)]
    aoc = [K.sbuf(f"C2_aoc{i}", (128, CC, TC), F32) for i in range(2)]
    xnc = [K.sbuf(f"C2_xnc{i}", (128, CC, TC), BF16) for i in range(2)]
    mrw = [K.sbuf(f"C2_mrw{i}", (1, 2 * TC), F32) for i in range(2)]
    fgw = [K.sbuf(f"C2_fgw{i}", (128, CC, 128), BF16) for i in range(4)]
    fuw = [K.sbuf(f"C2_fuw{i}", (128, CC, 128), BF16) for i in range(4)]
    fdw = [K.sbuf(f"C2_fdw{i}", (128, NHT, 128), BF16) for i in range(2)]
    tinf = K.sbuf("C2_tinf", (128, CC, TC), F32)
    ln3f = K.sbuf("C2_ln3f", (128, CC, TC), F32)
    tb16 = K.sbuf("C2_tb16", (128, CC, TC), BF16)
    tsq = K.sbuf("C2_tsq", (128, CC, TC), BF16)
    sg16c = K.sbuf("C2_sg16c", (128, 2, 512), BF16)
    h16c = K.sbuf("C2_h16c", (128, NHT, 512), BF16)
    srow2 = K.sbuf("C2_srow2", (1, 2 * TC), F32)
    vrow2 = K.sbuf("C2_vrow2", (1, 3 * TC), F32)
    PD = K.psum("C2_PD", (128, 2048), F32)
    PGU = K.psum("C2_PGU", (128, 1536), F32)
    K.dma("sync", gam1[:], d["gam1"][:], "c2CST")
    g1_pt = K.dma("sync", bet1[:], d["bet1"][:], "c2CST")

    ldq = [None, None]
    used = [None, None]
    wg_used = [None] * 4
    wd_used = [None, None]
    ev_stats = None
    prev_bc2 = None
    ev_gu = None
    hm_pt = None
    sg_prevs = [None, None]
    hm_prevs = [None, None]
    sgp = [None, None]
    ev_fo = None
    st_out = None
    prev_x2 = None
    psq2 = None
    for c in range(NCH):
        cb = c % 2
        tsl = slice(c * TC, (c + 1) * TC)
        if c == 0:
            K.dma("sync", x1c[0][:], x1_pct[:, :, tsl], "c2G0", wait=[bsx])
            K.dma("sync", aoc[0][:], aout_pct[:, :, tsl], "c2G0",
                  wait=[("cAOE0", K.cnt["cAOE0"]), ("cAOE1", K.cnt["cAOE1"])])
            K.dma("sync", xnc[0][:], xn1_pct[:, :, tsl], "c2G0",
                  wait=[("cSTx", K.cnt["cSTx"])])
            K.dma("sync", mrw[0][0:1, 0:TC], d["s_mr1"][0:1, tsl], "c2G0",
                  wait=[("cSTm", K.cnt["cSTm"])])
            ldq[0] = K.dma("sync", mrw[0][0:1, TC:2 * TC], d["s_mr1"][1:2, tsl], "c2G0")
        if c + 1 < NCH:
            nsl = slice((c + 1) * TC, (c + 2) * TC)
            nb_ = (c + 1) % 2
            w_ = [used[nb_], bsx, ("cAOE0", K.cnt["cAOE0"]),
                  ("cAOE1", K.cnt["cAOE1"]), ("cSTx", K.cnt["cSTx"]),
                  ("cSTm", K.cnt["cSTm"])]
            K.dma("sync", x1c[nb_][:], x1_pct[:, :, nsl], f"c2G{nb_}", wait=w_)
            K.dma("sync", aoc[nb_][:], aout_pct[:, :, nsl], f"c2G{nb_}")
            K.dma("sync", xnc[nb_][:], xn1_pct[:, :, nsl], f"c2G{nb_}")
            K.dma("sync", mrw[nb_][0:1, 0:TC], d["s_mr1"][0:1, nsl], f"c2G{nb_}")
            ldq[nb_] = K.dma("sync", mrw[nb_][0:1, TC:2 * TC], d["s_mr1"][1:2, nsl],
                             f"c2G{nb_}")
        b3 = K.mm(PD[:, 0:512], onesf[:], mrw[cb][0:1, 0:TC], True, True,
                  wait=[ldq[cb], ("VE", K.cnt["VE"])])
        b3 = K.mm(PD[:, 512:1024], onesf[:], mrw[cb][0:1, TC:2 * TC], True, True,
                  inc=True)
        m3v = PD[:, 0:512][:, None, :].broadcast_to((128, CC, 512))
        r3v = PD[:, 512:1024][:, None, :].broadcast_to((128, CC, 512))
        K.op("vector", "tensor_sub", wait=[b3, ldq[cb], st_out], out=tinf[:],
             in0=x1c[cb][:], in1=m3v)
        K.op("vector", "tensor_mul", out=tinf[:], in0=tinf[:], in1=r3v)
        tin_pt = K.op("vector", "tensor_add", inc=True, out=tinf[:],
                      in0=tinf[:], in1=aoc[cb][:])
        tb_pt = K.op("vector", "tensor_copy", out=tb16[:], in_=tinf[:])
        psq2 = K.act(tsq[:], tinf[:], AF.Square, wait=[tin_pt], inc=True)
        for i in range(CC):
            K.mm(PGU[0:1, 0:512], onesb[:], tb16[:, i, :], start=(i == 0),
                 stop=(i == CC - 1),
                 wait=[tb_pt, ev_stats, sg_prevs[0], hm_prevs[0]])
        mmst = None
        for i in range(CC):
            mmst = K.mm(PGU[0:1, 512:1024], onesb[:], tsq[:, i, :], start=(i == 0),
                        stop=(i == CC - 1), wait=[psq2, sg_prevs[1]],
                        inc=(i == CC - 1))
        K.act(srow2[0:1, 0:TC], PGU[0:1, 0:512], AF.Identity, scale=1.0 / D,
              wait=[mmst, prev_bc2])
        ev_stats = K.act(srow2[0:1, TC:2 * TC], PGU[0:1, 512:1024], AF.Identity,
                         scale=1.0 / D, inc=True)
        K.op("vector", "tensor_mul", wait=[ev_stats], out=vrow2[0:1, 0:TC],
             in0=srow2[0:1, 0:TC], in1=srow2[0:1, 0:TC])
        vv2 = K.op("vector", "tensor_sub", inc=True, out=vrow2[0:1, 0:TC],
                   in0=srow2[0:1, TC:2 * TC], in1=vrow2[0:1, 0:TC])
        sd2 = K.act(vrow2[0:1, TC:2 * TC], vrow2[0:1, 0:TC], AF.Sqrt,
                    bias=ceps[0:1, :], wait=[vv2], inc=True)
        rr2 = K.op("vector", "reciprocal", wait=[sd2], inc=True,
                   out=vrow2[0:1, 2 * TC:3 * TC], in_=vrow2[0:1, TC:2 * TC])
        b4 = K.mm(PD[:, 1024:1536], onesf[:], srow2[0:1, 0:TC], True, True,
                  wait=[rr2, ("VE", K.cnt["VE"])])
        b4 = K.mm(PD[:, 1536:2048], onesf[:], vrow2[0:1, 2 * TC:3 * TC], True, True,
                  inc=True)
        prev_bc2 = b4
        m4v = PD[:, 1024:1536][:, None, :].broadcast_to((128, CC, 512))
        r4v = PD[:, 1536:2048][:, None, :].broadcast_to((128, CC, 512))
        K.op("vector", "tensor_sub", wait=[b4], out=ln3f[:], in0=tinf[:], in1=m4v)
        K.op("vector", "tensor_mul", out=ln3f[:], in0=ln3f[:], in1=r4v)
        ln_pt = None
        for i in range(CC):
            ln_pt = K.op("vector", "tensor_scalar", inc=(i == CC - 1),
                         wait=([g1_pt] if i == 0 else ()),
                         out=ln3f[:, i, :], in0=ln3f[:, i, :],
                         scalar1=gam1[:, i:i + 1], scalar2=bet1[:, i:i + 1],
                         op0=ALU.mult, op1=ALU.add, const_scalars=True)
        for ht in range(NHT):
            hb = ht % 4
            pp = ht % 2
            lg = K.dma("gpsimd", fgw[hb][:], d["s_fg"][ht], f"c2W{hb}",
                       wait=[wg_used[hb], wp["PREP"]])
            lu = K.dma("gpsimd", fuw[hb][:], d["s_fu"][ht], f"c2W{hb}")
            greg = PGU[:, pp * 512:pp * 512 + 512]
            ureg = PGU[:, 1024:1536]
            gm = None
            for i in range(CC):
                gm = K.mm(greg, fgw[hb][:, i, :], xnc[cb][:, i, :],
                          start=(i == 0), stop=(i == CC - 1),
                          wait=(lu, ldq[cb], sg_prevs[pp],
                                ev_stats if ht < 2 else None),
                          inc=(i == CC - 1))
            sg_pt = K.act(sg16c[:, pp, :], greg, AF.Silu, scale=DEQ,
                          wait=[gm, hm_prevs[pp]], inc=True)
            sg_prevs[pp] = sg_pt
            sgp[pp] = sg_pt
            um = None
            for i in range(CC):
                um = K.mm(ureg, fuw[hb][:, i, :], xnc[cb][:, i, :],
                          start=(i == 0), stop=(i == CC - 1),
                          wait=(hm_pt,), inc=(i == CC - 1))
            hm_pt = K.op("vector", "tensor_mul", wait=[um, sgp[pp]], inc=True,
                         out=h16c[:, ht, :], in0=sg16c[:, pp, :], in1=ureg)
            hm_prevs[pp] = hm_pt
            ev_gu = hm_pt
            wg_used[hb] = um
        for wave in range(2):
            for dtw in range(4):
                dt = wave * 4 + dtw
                db = dt % 2
                ldd = K.dma("gpsimd", fdw[db][:], d["s_fd"][dt], f"c2D{db}",
                            wait=[wd_used[db], wp["PREP"]])
                reg = PD[:, dtw * 512:(dtw + 1) * 512]
                dn = None
                for hc in range(NHT):
                    dn = K.mm(reg, fdw[db][:, hc, :], h16c[:, hc, :],
                              start=(hc == 0), stop=(hc == NHT - 1),
                              wait=(ldd, hm_pt, ln_pt, ev_fo),
                              inc=(hc == NHT - 1))
                wd_used[db] = dn
                ev_fo = K.op("vector", "scalar_tensor_tensor",
                             wait=[dn, ln_pt, st_out], inc=True,
                             out=tinf[:, dt, :], in0=reg, scalar=DEQ * DEQ,
                             in1=ln3f[:, dt, :], op0=ALU.mult, op1=ALU.add)
        x2_pt = K.op("vector", "tensor_add", wait=[ev_fo], inc=True, out=tinf[:],
                     in0=tinf[:], in1=x1c[cb][:])
        prev_x2 = x2_pt
        used[cb] = x2_pt
        st_out = K.dma("sync", out_pct[:, :, tsl], tinf[:], "oOUT", wait=[x2_pt])
    return {"out": True}


PHASES = ["A", "L0", "B", "C"]


def body(K, d):
    wp = phase_prep(K, d)
    st = {}
    if "A" in PHASES:
        st.update(phase_A(K, d, wp) or {})
    if "L0" in PHASES:
        st.update(phase_L0(K, d, wp, st) or {})
    if "B" in PHASES:
        st.update(phase_B(K, d, wp, st) or {})
    if "C" in PHASES:
        st.update(phase_C(K, d, wp, st) or {})
    if "out" not in st:
        z = K.sbuf("zz_out", (128, 64), F32)
        zp = K.op("vector", "memset", z[:], 0.0, inc=True)
        K.dma("sync", d["out"][:].rearrange("c p t -> p c t")[:, 0:1, 0:64], z[:, None, :], "oOUT", wait=[zp])
    K.barrier()


RACECHECK = True


def build_program():
    nc = bass.Bass("TRN2", target_bir_lowering=False,
                   detect_race_conditions=RACECHECK)
    d = declare_tensors(nc)
    es = contextlib.ExitStack()
    K = KB(nc, es)
    for s in ALLSEMS:
        K.semh[s] = es.enter_context(nc.semaphore(s))
    K.cur = "__dry__"
    K.walk_reset()
    body(K, d)
    with nc.Block() as block:
        for engname, deco in (("sync", block.sync), ("gpsimd", block.gpsimd),
                              ("vector", block.vector), ("scalar", block.scalar),
                              ("tensor", block.tensor)):
            def runner(eng, engname=engname):
                K.cur = engname
                K.eng = eng
                K.walk_reset()
                body(K, d)
            deco(runner)
    es.close()
    return nc


def make_in_maps(inputs):
    h = host_prep(inputs)
    x = np.asarray(inputs["L0_attn_x"], dtype=np.float32)
    in_maps = []
    for c in range(NCORES):
        m = dict(h)
        xc = np.concatenate([x[2 * c].T, x[2 * c + 1].T], axis=1)
        m["xT"] = np.ascontiguousarray(xc.reshape(CC, 128, T))
        in_maps.append(m)
    return in_maps


def kernel(**inputs):
    nc = build_program()
    in_maps = make_in_maps(inputs)
    res = run_bass_kernel_spmd(nc, in_maps, core_ids=list(range(NCORES)))
    outs = []
    for c in range(NCORES):
        oT = np.asarray(res.results[c]["out"]).reshape(D, T)
        outs.append(oT[:, 0:S].T)
        outs.append(oT[:, S:].T)
    return np.stack(outs).astype(np.float32)
